# version 23 (final)
# baseline (speedup 1.0000x reference)
import contextlib
import numpy as np
import concourse.bass as bass
import concourse.mybir as mybir
from concourse.bass_utils import run_bass_kernel_spmd

F32 = mybir.dt.float32
BF16 = mybir.dt.bfloat16
AF = mybir.ActivationFunctionType
ALU = mybir.AluOpType
AX = mybir.AxisListType
AP = bass.AP

L, D = 2048, 1024
NT, DC = 16, 8
H, HD = 16, 64
CL, CLC = 256, 2
F, FC = 2048, 16
E = 8
EPS = 1e-5
NEG = -1e30
CSH = 44.0
SBUF_BASE = 16512
SBUF_TOP = 229344

ENGS = ("te", "ve", "se", "ge", "sy")


class Tok:
    __slots__ = ("sem", "val")

    def __init__(self, sem, val):
        self.sem, self.val = sem, val


class Sched:

    def __init__(self, nc, stack):
        self.nc = nc
        self.ops = {e: [] for e in ENGS}
        self.sem = {}
        for e in ("te", "ve", "se", "ge"):
            self.sem[e] = stack.enter_context(nc.semaphore("s_" + e))
        self.cnt = {e: 0 for e in ("te", "ve", "se", "ge")}
        self.fam = {}
        for name, q in (("ld", "sy"), ("ld2", "sy"), ("st", "sy"), ("kv", "sy"),
                        ("wt", "ge")):
            sems = [stack.enter_context(nc.semaphore(f"{name}{i}")) for i in range(2)]
            self.fam[name] = (q, sems, [0, 0])
        self.cc_sem = stack.enter_context(nc.semaphore("scc"))
        self.cc_cnt = 0

    def op(self, eng, fn, deps=(), inc=True):
        tok = None
        if inc:
            self.cnt[eng] += 1
            tok = Tok(self.sem[eng], self.cnt[eng])
        self.ops[eng].append((fn, list(deps), tok))
        return tok

    def dma(self, fam, fn, deps=()):
        q, sems, cnts = self.fam[fam]
        i = (cnts[0] + cnts[1]) % 2
        deps = list(deps)
        if cnts[i]:
            deps.append(Tok(sems[i], 16 * cnts[i]))
        cnts[i] += 1
        tok = Tok(sems[i], 16 * cnts[i])
        self.ops[q].append((fn, deps, Tok(sems[i], None)))
        return tok

    def collective(self, fn, deps=()):
        self.cc_cnt += 1
        tok = Tok(self.cc_sem, self.cc_cnt)
        self.ops["ge"].append((fn, list(deps), Tok(self.cc_sem, None)))
        return tok

    def emit(self, blk):
        bodies = {"te": blk.tensor, "ve": blk.vector, "se": blk.scalar,
                  "ge": blk.gpsimd, "sy": blk.sync}
        for eng in ENGS:
            def body(e, eng=eng):
                hw = {}
                for fn, deps, tok in self.ops[eng]:
                    for d in deps:
                        if d is None:
                            continue
                        key = d.sem.name
                        if hw.get(key, 0) >= d.val:
                            continue
                        e.wait_ge(d.sem, d.val)
                        hw[key] = d.val
                    ins = fn(e)
                    if tok is not None:
                        if tok.val is None:
                            amt = 16 if tok.sem is not self.cc_sem else 1
                            ins.then_inc(tok.sem, amt)
                        else:
                            ins.then_inc(tok.sem, 1)
            bodies[eng](body)


class Alloc:

    def __init__(self, nc):
        self.nc = nc
        self.free = [[SBUF_BASE, SBUF_TOP - SBUF_BASE, []]]
        self.live = {}
        self.guards = {}

    def alloc(self, name, shape, dtype):
        size = int(np.prod(shape[1:])) * (4 if dtype == F32 else 2)
        size = (size + 63) & ~63
        for i, (off, flen, g) in enumerate(self.free):
            if flen >= size:
                if flen == size:
                    self.free.pop(i)
                else:
                    self.free[i] = [off + size, flen - size, list(g)]
                t = self.nc.alloc_sbuf_tensor_at(name, shape, dtype, offset=off)
                self.live[name] = (off, size)
                self.guards[name] = list(g)
                return t
        raise MemoryError(f"SBUF alloc fail {name} {size}B free={self.free}")

    def drop(self, *names, after=()):
        after = [a for a in (after if isinstance(after, (list, tuple)) else [after])
                 if a is not None]
        for n in names:
            off, size = self.live.pop(n)
            self.free.append([off, size, list(after)])
            self.guards.pop(n, None)
        self.free.sort()
        out = []
        for off, ln, g in self.free:
            if out and out[-1][0] + out[-1][1] == off:
                out[-1][1] += ln
                out[-1][2] = out[-1][2] + g
            else:
                out.append([off, ln, list(g)])
        self.free = out


def build_program(upto="full", dbg=()):
    nc = bass.Bass("TRN2", target_bir_lowering=False)
    stack = contextlib.ExitStack()
    al = Alloc(nc)
    dbg = list(dbg)

    di = lambda n, s: nc.dram_tensor(n, s, F32, kind="ExternalInput").ap()
    x = di("x", [L, D])
    wq = di("wq", [D, D])
    wkv = di("wkv", [D, CL])
    wkup = di("wkup", [CL, D])
    wvup = di("wvup", [CL, D])
    wout = di("wout", [D, D])
    wipT = di("wipT", [D, 3 * D])
    wopT = di("wopT", [D, D])
    wrT = di("wrT", [D, E])
    w1 = di("w1", [E, D, F])
    w2 = di("w2", [E, F, D])
    ln1w = di("ln1w", [D])
    ln1b = di("ln1b", [D])
    ln2w = di("ln2w", [D])
    ln2b = di("ln2b", [D])
    ipb = di("ipb", [3 * D])
    opb = di("opb", [D])
    tmask = di("tmask", [E])
    out = nc.dram_tensor("out", [L, D], F32, kind="ExternalOutput").ap()
    ag_in = nc.dram_tensor("ag_in", [L, 2 * D], BF16).ap()
    ag_out = nc.dram_tensor("ag_out", [8 * L, 2 * D], BF16).ap()
    x1_sp = nc.dram_tensor("x1_sp", [L, D], F32).ap()
    q_sp = nc.dram_tensor("q_sp", [L, D], BF16).ap()
    acc_sp = nc.dram_tensor("acc_sp", [L, D], F32).ap()
    s_sp = nc.dram_tensor("s_sp", [L, D], F32).ap()

    pb = [nc.alloc_psum_tensor(f"pb{i}", [128, 512], F32) for i in range(6)]
    pbT = [nc.alloc_psum_tensor(f"pbT{i}", [128, 512], BF16) for i in range(2)]

    with stack:
        S = Sched(nc, stack)
        blk = stack.enter_context(nc.Block())

        stats = al.alloc("stats", [128, 32], F32)
        rstd0 = al.alloc("rstd0", [128, NT], F32)
        nmr0 = al.alloc("nmr0", [128, NT], F32)
        ident_f = al.alloc("ident_f", [128, 128], F32)
        ident_b = al.alloc("ident_b", [128, 128], BF16)

        tmaskrep = al.alloc("tmaskrep", [128, 128], F32)
        ones_b = al.alloc("ones_b", [128, 128], BF16)
        tri_b = al.alloc("tri_b", [128, 128], BF16)
        ones_f = al.alloc("ones_f", [128, 128], F32)
        gates = al.alloc("gates", [128, NT * E], F32)
        scr = al.alloc("scr", [128, D], F32)
        scr2 = al.alloc("scr2", [128, D], F32)
        dumacc = al.alloc("dumacc", [128, 1], F32)
        cshb = al.alloc("cshb", [128, 1], F32)
        rstd1 = al.alloc("rstd1", [128, NT], F32)
        nmr1 = al.alloc("nmr1", [128, NT], F32)
        rstd2 = al.alloc("rstd2", [128, NT], F32)
        nmr2 = al.alloc("nmr2", [128, NT], F32)

        dbg_marks = {}

        def dump(name, t):
            if name in dbg:
                dbg_marks[name] = (t, nc.dram_tensor(
                    "dbg_" + name, [128, int(t.shape[1])],
                    t.dtype, kind="ExternalOutput").ap())

        t_ones_b = S.op("ge", lambda e: e.memset(ones_b[:, :], 1.0))
        t_ones_f = S.op("ge", lambda e: e.memset(ones_f[:, :], 1.0))

        t_ib = S.op("ge", lambda e: e.affine_select(
            ident_b[:, :], ones_b[:, :], [[1, 128]], ALU.is_equal, 0.0,
            base=0, channel_multiplier=-1), deps=[t_ones_b])
        t_if = S.op("ge", lambda e: e.affine_select(
            ident_f[:, :], ones_f[:, :], [[1, 128]], ALU.is_equal, 0.0,
            base=0, channel_multiplier=-1), deps=[t_ones_f])
        al.drop("ones_f", after=[t_if])
        t_zb = S.op("ge", lambda e: e.memset(tri_b[:, :], 0.0))
        t_trib = S.op("ge", lambda e: e.affine_select(
            tri_b[:, :], tri_b[:, :], [[1, 128]], ALU.is_ge, NEG,
            base=0, channel_multiplier=-1), deps=[t_zb])
        t_tm = S.dma("ld2", lambda e: e.dma_start(
            out=tmaskrep[:, :], in_=AP(tmask.tensor, 0, [[0, 128], [0, 16], [1, 8]])))
        t_csh = S.op("ge", lambda e: e.memset(cshb[:, :], -CSH))

        def ln_pass(tag, src_tok, rstd, nmr, tile_deps, normalize=False,
                    gamma=None, beta=None, out_tok=None, out_guards=()):
            sq_toks = []
            prev = []
            for t in range(NT):
                tk = S.op("se", lambda e, t=t: e.activation(
                    scr[:, :], src_tok[:, t * D:(t + 1) * D], AF.Square,
                    accum_out=stats[:, 16 + t:17 + t]),
                    deps=list(tile_deps[t]) + prev)
                prev = [tk]
                sq_toks.append(tk)
            rd_toks = []
            for t in range(NT):
                rd_toks.append(S.op("ve", lambda e, t=t: e.reduce_sum(
                    stats[:, t:t + 1], src_tok[:, t * D:(t + 1) * D], axis=AX.X),
                    deps=tile_deps[t]))
            sums = stats[:, 0:16]
            sumsq = stats[:, 16:32]
            t1 = S.op("ve", lambda e: e.tensor_scalar_mul(sums[:, :], sums[:, :], 1.0 / D),
                      deps=[rd_toks[-1]])
            t2 = S.op("ve", lambda e: e.tensor_scalar(
                sumsq[:, :], sumsq[:, :], 1.0 / D, EPS, ALU.mult, ALU.add),
                deps=[sq_toks[-1], t1])
            t3 = S.op("ve", lambda e: e.tensor_mul(rstd[:, :], sums[:, :], sums[:, :]),
                      deps=[t1])
            t4 = S.op("ve", lambda e: e.tensor_sub(sumsq[:, :], sumsq[:, :], rstd[:, :]),
                      deps=[t2, t3])
            t5 = S.op("se", lambda e: e.activation(scr[:, 0:16], sumsq[:, :], AF.Sqrt),
                      deps=[t4, sq_toks[-1]])
            t6 = S.op("ve", lambda e: e.reciprocal(rstd[:, :], scr[:, 0:16]), deps=[t5])
            t7 = S.op("ve", lambda e: e.scalar_tensor_tensor(
                nmr[:, :], sums[:, :], -1.0, rstd[:, :], ALU.mult, ALU.mult), deps=[t6])
            outs = []
            if normalize:
                for t in range(NT):
                    outs.append(S.op("ve", lambda e, t=t: e.tensor_scalar(
                        src_tok[:, t * D:(t + 1) * D], src_tok[:, t * D:(t + 1) * D],
                        rstd[:, t:t + 1], nmr[:, t:t + 1], ALU.mult, ALU.add),
                        deps=[t7, t5]))
            if out_tok is not None:
                for t in range(NT):
                    buf = scr if t % 2 == 0 else scr2
                    a = S.op("ve", lambda e, t=t, buf=buf: e.tensor_scalar(
                        buf[:, :], src_tok[:, t * D:(t + 1) * D],
                        rstd[:, t:t + 1], nmr[:, t:t + 1], ALU.mult, ALU.add),
                        deps=[t7, t5] + ([outs[t - 2]] if t >= 2 else []))
                    b = S.op("ve", lambda e, buf=buf: e.tensor_mul(
                        buf[:, :], buf[:, :], gamma[:, :]), deps=[a])
                    c = S.op("ve", lambda e, t=t, buf=buf: e.tensor_add(
                        out_tok[:, t * D:(t + 1) * D], buf[:, :], beta[:, :]),
                        deps=[b] + list(out_guards))
                    outs.append(c)
            return outs, t7, t5

        def mm_pipe(units, mk_mms, mk_evac, extra_mm_deps=lambda u: [],
                    extra_ev_deps=lambda u: [], banks=(0, 1, 2), eng_ev="ve"):
            ev_toks = []
            mm_toks = []
            nb = len(banks)
            for n, u in enumerate(units):
                ps = pb[banks[n % nb]]
                deps = list(extra_mm_deps(u))
                if n >= nb:
                    deps.append(ev_toks[n - nb])
                mm_toks.append(S.op("te", lambda e, u=u, ps=ps: mk_mms(e, u, ps),
                                    deps=deps))
                ev_toks.append(S.op(eng_ev, lambda e, u=u, ps=ps: mk_evac(e, u, ps),
                                    deps=[mm_toks[-1]] + list(extra_ev_deps(u))))
            return mm_toks, ev_toks

        def wload(dst, srct, rows, cols, deps=()):
            kc = rows // 128
            return S.dma("wt", lambda e: e.dma_start(
                out=AP(dst, 0, [[kc * cols, 128], [cols, kc], [1, cols]]),
                in_=AP(srct.tensor, 0, [[cols, 128], [128 * cols, kc], [1, cols]])),
                deps=deps)

        xn_tok = al.alloc("xn_tok", [128, NT * D], F32)
        xn_fm = al.alloc("xn_fm", [128, DC * L], BF16)
        xload = []
        for h in range(2):
            xload.append(S.dma("ld", lambda e, h=h: e.dma_start(
                out=AP(xn_tok, h * 8 * D, [[NT * D, 128], [D, 8], [1, D]]),
                in_=AP(x.tensor, h * 8 * 128 * D, [[D, 128], [128 * D, 8], [1, D]]))))
        tdeps = [[xload[t // 8]] for t in range(NT)]
        nrm0, _, ln0_t5 = ln_pass("ln0", xn_tok, rstd0, nmr0, tdeps, normalize=True)

        xnfm_ev = {}
        xnfm_tp = {}
        for t in range(NT):
            for dc in range(DC):
                n = t * DC + dc
                deps = [nrm0[t], t_if]
                if n >= 2:
                    deps.append(xnfm_ev[n - 2])
                xnfm_tp[n] = S.op("te", lambda e, t=t, dc=dc, n=n: e.transpose(
                    pb[4 + n % 2][:, 0:128],
                    xn_tok[:, t * D + dc * 128: t * D + (dc + 1) * 128],
                    ident_f[:, :]), deps=deps)
                xnfm_ev[n] = S.op("ve", lambda e, t=t, dc=dc, n=n: e.tensor_copy(
                    xn_fm[:, dc * L + t * 128: dc * L + (t + 1) * 128],
                    pb[4 + n % 2][:, 0:128]), deps=[xnfm_tp[n]])
        t_xnfm = xnfm_ev[NT * DC - 1]
        dump("xn_tok", xn_tok)
        dump("xn_fm", xn_fm)
        if upto == "ph0":
            return _finish(S, blk, dbg_marks)

        latent = al.alloc("latent", [128, CLC * L], BF16)
        Q_fm = al.alloc("Q_fm", [128, DC * L], BF16)
        K_fm = al.alloc("K_fm", [128, DC * L], BF16)
        wq_sb = al.alloc("wq_sb", [128, DC * D], BF16)
        wkv_sb = al.alloc("wkv_sb", [128, DC * CL], BF16)
        wkup_sb = al.alloc("wkup_sb", [128, CLC * D], BF16)
        wvup_sb = al.alloc("wvup_sb", [128, CLC * D], BF16)

        t_wkv = wload(wkv_sb, wkv, D, CL)
        t_wq = wload(wq_sb, wq, D, D)
        t_wkup = wload(wkup_sb, wkup, CL, D)
        t_wvup = wload(wvup_sb, wvup, CL, D)

        def lat_mms(e, u, ps):
            mc, tt = u
            for dc in range(DC):
                ins = e.matmul(ps[:, 0:512],
                               wkv_sb[:, dc * CL + mc * 128: dc * CL + (mc + 1) * 128],
                               xn_fm[:, dc * L + tt * 512: dc * L + (tt + 1) * 512],
                               start=(dc == 0), stop=(dc == DC - 1))
            return ins

        def lat_ev(e, u, ps):
            mc, tt = u
            return e.tensor_copy(latent[:, mc * L + tt * 512: mc * L + (tt + 1) * 512],
                                 ps[:, 0:512])

        lat_mm, lat_evt = mm_pipe([(mc, tt) for mc in range(CLC) for tt in range(4)],
                                  lat_mms, lat_ev,
                                  extra_mm_deps=lambda u: [t_wkv,
                                                           xnfm_ev[(4 * u[1] + 4) * DC - 1]])

        def q_mms(e, u, ps):
            mc, tt = u
            for dc in range(DC):
                ins = e.matmul(ps[:, 0:512],
                               wq_sb[:, dc * D + mc * 128: dc * D + (mc + 1) * 128],
                               xn_fm[:, dc * L + tt * 512: dc * L + (tt + 1) * 512],
                               start=(dc == 0), stop=(dc == DC - 1))
            return ins

        def q_ev(e, u, ps):
            mc, tt = u
            return e.tensor_scalar_mul(Q_fm[:, mc * L + tt * 512: mc * L + (tt + 1) * 512],
                                       ps[:, 0:512], 0.125)

        q_mm, q_evt = mm_pipe([(mc, tt) for mc in range(DC) for tt in range(4)],
                              q_mms, q_ev,
                              extra_mm_deps=lambda u: [t_wq, lat_evt[-1],
                                                       xnfm_ev[(4 * u[1] + 4) * DC - 1]])

        def k_mms(e, u, ps):
            mc, tt = u
            for cc in range(CLC):
                ins = e.matmul(ps[:, 0:512],
                               wkup_sb[:, cc * D + mc * 128: cc * D + (mc + 1) * 128],
                               latent[:, cc * L + tt * 512: cc * L + (tt + 1) * 512],
                               start=(cc == 0), stop=(cc == CLC - 1))
            return ins

        def k_ev(e, u, ps):
            mc, tt = u
            return e.tensor_copy(K_fm[:, mc * L + tt * 512: mc * L + (tt + 1) * 512],
                                 ps[:, 0:512])

        k_mm, k_evt = mm_pipe([(mc, tt) for mc in range(DC) for tt in range(4)],
                              k_mms, k_ev,
                              extra_mm_deps=lambda u: [t_wkup, lat_evt[-1], q_evt[-1]])
        dump("latent", latent)
        dump("Q_fm", Q_fm)
        dump("K_fm", K_fm)
        if upto == "ph1":
            return _finish(S, blk, dbg_marks)

        al.drop("xn_fm", after=[q_mm[-1]])
        al.drop("wq_sb", after=[q_mm[-1]])
        al.drop("wkv_sb", after=[lat_mm[-1]])
        al.drop("wkup_sb", after=[k_mm[-1]])
        hn_tok = al.alloc("hn_tok", [128, NT * D], BF16)
        g1 = al.alloc("g1rep", [128, D], F32)
        b1 = al.alloc("b1rep", [128, D], F32)
        t_g1 = S.dma("ld2", lambda e: e.dma_start(
            out=g1[:, :], in_=AP(ln1w.tensor, 0, [[0, 128], [1, D]])),
            deps=al.guards["g1rep"])
        t_b1 = S.dma("ld2", lambda e: e.dma_start(
            out=b1[:, :], in_=AP(ln1b.tensor, 0, [[0, 128], [1, D]])),
            deps=al.guards["b1rep"])
        a1 = S.op("ve", lambda e: e.tensor_scalar(
            rstd1[:, :], stats[:, 16:32], 1.0 + EPS, -EPS, ALU.mult, ALU.add),
            deps=[nrm0[0]])
        a1b = S.op("ve", lambda e: e.reciprocal(rstd1[:, :], rstd1[:, :]), deps=[a1])
        a2 = S.op("ve", lambda e: e.tensor_mul(
            rstd1[:, :], stats[:, 16:32], rstd1[:, :]), deps=[a1b])
        a3 = S.op("se", lambda e: e.activation(rstd1[:, :], rstd1[:, :], AF.Sqrt),
                  deps=[a2])
        hn_outs = []
        gd_hntok = al.guards["hn_tok"]
        for t in range(NT):
            buf = scr if t % 2 == 0 else scr2
            ha = S.op("ve", lambda e, t=t, buf=buf: e.scalar_tensor_tensor(
                buf[:, :], xn_tok[:, t * D:(t + 1) * D],
                rstd1[:, t:t + 1], g1[:, :], ALU.mult, ALU.mult),
                deps=[a3, nrm0[t], t_g1, t_b1] +
                     ([hn_outs[t - 2]] if t >= 2 else []))
            hc = S.op("ve", lambda e, t=t, buf=buf: e.tensor_add(
                hn_tok[:, t * D:(t + 1) * D], buf[:, :], b1[:, :]),
                deps=[ha] + gd_hntok)
            hn_outs.append(hc)
        al.drop("g1rep", "b1rep", after=[hn_outs[-1]])
        al.drop("xn_tok", after=[hn_outs[-1]])
        hn_fm = al.alloc("hn_fm", [128, DC * L], BF16)
        gd_hnfm = al.guards["hn_fm"]
        hnfm_ev = {}
        hnfm_tp = {}
        for t in range(NT):
            for dc in range(DC):
                n = t * DC + dc
                deps = [hn_outs[t], t_ib]
                if n >= 2:
                    deps.append(hnfm_ev[n - 2])
                hnfm_tp[n] = S.op("te", lambda e, t=t, dc=dc, n=n: e.transpose(
                    pbT[n % 2][:, 0:128],
                    hn_tok[:, t * D + dc * 128: t * D + (dc + 1) * 128],
                    ident_b[:, :]), deps=deps)
                hnfm_ev[n] = S.op("ve", lambda e, t=t, dc=dc, n=n: e.tensor_copy(
                    hn_fm[:, dc * L + t * 128: dc * L + (t + 1) * 128],
                    pbT[n % 2][:, 0:128]),
                    deps=[hnfm_tp[n]] + gd_hnfm)
        t_hnfm = hnfm_ev[NT * DC - 1]
        al.drop("hn_tok", after=[hnfm_tp[NT * DC - 1]])

        wip_sb = al.alloc("wip_sb", [128, DC * 3 * D], BF16)
        ipb_rep = al.alloc("ipb_rep", [128, 3 * D], F32)
        ipbq_rep = al.alloc("ipbq_rep", [128, D], F32)
        qt = [al.alloc(f"qt{i}", [128, D], BF16) for i in range(2)]
        kvt = [al.alloc(f"kvt{i}", [128, 2 * D], BF16) for i in range(2)]
        t_wip = S.dma("wt", lambda e: e.dma_start(
            out=AP(wip_sb, 0, [[DC * 3 * D, 128], [3 * D, DC], [1, 3 * D]]),
            in_=AP(wipT.tensor, 0, [[3 * D, 128], [128 * 3 * D, DC], [1, 3 * D]])),
            deps=al.guards["wip_sb"])
        t_ipb = S.dma("ld2", lambda e: e.dma_start(
            out=ipb_rep[:, :], in_=AP(ipb.tensor, 0, [[0, 128], [1, 3 * D]])),
            deps=al.guards["ipb_rep"])
        t_ipbq = S.op("ve", lambda e: e.tensor_scalar_mul(
            ipbq_rep[:, :], ipb_rep[:, 0:D], 0.125),
            deps=[t_ipb] + al.guards["ipbq_rep"])
        gd_qtok = [al.guards["qt0"], al.guards["qt1"]]
        gd_kvt = [al.guards["kvt0"], al.guards["kvt1"]]
        kv_store = {}
        q_store = {}

        def ip_mms(e, u, ps):
            tb, sl = u
            for dc in range(DC):
                ins = e.matmul(ps[:, 0:512],
                               hn_fm[:, dc * L + tb * 128: dc * L + (tb + 1) * 128],
                               wip_sb[:, dc * 3 * D + sl * 512: dc * 3 * D + (sl + 1) * 512],
                               start=(dc == 0), stop=(dc == DC - 1))
            return ins

        def ip_ev(e, u, ps):
            tb, sl = u
            if sl < 2:
                dst = qt[tb % 2][:, sl * 512:(sl + 1) * 512]
                return e.scalar_tensor_tensor(dst, ps[:, 0:512], 0.125,
                                              ipbq_rep[:, sl * 512:(sl + 1) * 512],
                                              ALU.mult, ALU.add)
            dst = kvt[tb % 2][:, (sl - 2) * 512:(sl - 1) * 512]
            return e.scalar_tensor_tensor(dst, ps[:, 0:512], 1.0,
                                          ipb_rep[:, sl * 512:(sl + 1) * 512],
                                          ALU.mult, ALU.add)

        def ip_evdeps(u):
            tb, sl = u
            d = [t_ipb, t_ipbq]
            d += gd_qtok[tb % 2] if sl < 2 else gd_kvt[tb % 2]
            if sl == 0 and tb >= 2:
                d.append(q_store[tb - 2])
            if sl == 2 and tb >= 2:
                d.append(kv_store[tb - 2])
            return d

        ip_mm = []
        ip_evt = []
        ip_order = [(tb, sl) for tb in range(NT) for sl in (2, 3, 4, 5)] + \
                   [(tb, sl) for tb in range(NT) for sl in (0, 1)]
        for n, u in enumerate(ip_order):
            tb, sl = u
            ps = pb[n % 3]
            deps = [t_wip, hnfm_ev[(tb + 1) * DC - 1], k_evt[-1]]
            if n >= 3:
                deps.append(ip_evt[n - 3])
            ip_mm.append(S.op("te", lambda e, u=u, ps=ps: ip_mms(e, u, ps), deps=deps))
            ip_evt.append(S.op("ve", lambda e, u=u, ps=ps: ip_ev(e, u, ps),
                               deps=[ip_mm[-1]] + list(ip_evdeps(u))))
            if sl == 1:
                q_store[tb] = S.dma("st", lambda e, tb=tb: e.dma_start(
                    out=q_sp[tb * 128:(tb + 1) * 128, :], in_=qt[tb % 2][:, :]),
                    deps=[ip_evt[-1]])
            if sl == 5:
                kv_store[tb] = S.dma("st", lambda e, tb=tb: e.dma_start(
                    out=ag_in[tb * 128:(tb + 1) * 128, :], in_=kvt[tb % 2][:, :]),
                    deps=[ip_evt[-1]])
        al.drop("hn_fm", after=[ip_mm[-1]])
        t_ag = S.collective(lambda e: e.collective_compute(
            "AllGather", ALU.bypass, replica_groups=[list(range(8))],
            ins=[ag_in.opt()], outs=[ag_out.opt()]),
            deps=[kv_store[tb] for tb in range(NT)])
        al.drop("wip_sb", after=[ip_mm[-1]])
        al.drop("ipb_rep", "ipbq_rep", after=[ip_evt[-1]])
        al.drop("kvt0", "kvt1", after=[kv_store[14], kv_store[15]])
        al.drop("qt0", "qt1", after=[q_store[14], q_store[15]])
        dump("hn_fm", hn_fm)
        if upto == "ph1b":
            return _finish(S, blk, dbg_marks)

        V_sb = al.alloc("V_sb", [128, NT * D], BF16)
        gd_v = al.guards["V_sb"]
        y_fm = al.alloc("y_fm", [128, DC * L], BF16)
        gd_yfm = al.guards["y_fm"]
        def v_mms(e, u, ps):
            tb, half = u
            for cc in range(CLC):
                ins = e.matmul(ps[:, 0:512],
                               latent[:, cc * L + tb * 128: cc * L + (tb + 1) * 128],
                               wvup_sb[:, cc * D + half * 512: cc * D + (half + 1) * 512],
                               start=(cc == 0), stop=(cc == CLC - 1))
            return ins

        def v_ev(e, u, ps):
            tb, half = u
            return e.tensor_copy(V_sb[:, tb * D + half * 512: tb * D + (half + 1) * 512],
                                 ps[:, 0:512])

        v_mm, v_evt = mm_pipe([(tb, half) for tb in range(NT) for half in range(2)],
                              v_mms, v_ev,
                              extra_mm_deps=lambda u: [t_wvup, lat_evt[-1], k_evt[-1],
                                                       ip_evt[-1]],
                              extra_ev_deps=lambda u: gd_v)
        t_vall = v_evt[-1]
        al.drop("latent", after=[v_mm[-1]])
        al.drop("wvup_sb", after=[v_mm[-1]])
        dump("V_sb", V_sb)

        kvk = al.alloc("kvk", [128, 4 * D], BF16)
        kvr = al.alloc("kvr", [128, 8 * D], BF16)
        qbp = [al.alloc(f"qbp{i}", [128, D], BF16) for i in range(2)]
        tmpg = [al.alloc(f"tmpg{i}", [128, 2 * D], BF16) for i in range(2)]
        scp = [al.alloc(f"scp{i}", [128, 128], F32) for i in range(2)]
        smp = al.alloc("smp", [128, 32], F32)
        tmpv = [al.alloc(f"tmpv{i}", [128, D], BF16) for i in range(2)]
        accp = [al.alloc(f"accp{i}", [128, D], F32) for i in range(2)]
        gd_pump = {n: al.guards[n] for n in
                   ("kvr", "kvk", "qbp0", "qbp1", "tmpg0", "tmpg1", "scp0",
                    "scp1", "smp", "tmpv0", "tmpv1", "accp0", "accp1")}
        acc_store = {}

        def mk_kload(lt, half, deps):
            return S.dma("kv", lambda e, lt=lt, half=half: e.dma_start(
                out=AP(kvk, 0, [[4 * D, 128], [D, 4], [1, D]]),
                in_=AP(ag_out.tensor, (4 * half * L + lt * 128) * 2 * D,
                       [[2 * D, 128], [L * 2 * D, 4], [1, D]])), deps=deps)

        def pump_gen():
            prev_mul = {}
            prev_red = {}
            prev_pvm = {}
            prev_pva = {}
            prev_norm = {}
            prev_acst = {}
            for lt in range(NT):
                sc4 = scp[lt % 2]
                t_q = S.dma("ld", lambda e, lt=lt: e.dma_start(
                    out=qbp[lt % 2][:, :], in_=q_sp[lt * 128:(lt + 1) * 128, :]),
                    deps=[q_store[lt]] + gd_pump[f"qbp{lt % 2}"] +
                         ([prev_mul[lt - 2]] if lt >= 2 else []))
                kvds = {}
                kvds[0] = mk_kload(lt, 0, [t_ag] + gd_pump["kvk"] +
                                   ([prev_mul[lt - 1]] if lt >= 1 else []))
                t_vld = S.dma("kv", lambda e, lt=lt: e.dma_start(
                    out=AP(kvr, 0, [[8 * D, 128], [D, 8], [1, D]]),
                    in_=AP(ag_out.tensor, lt * 128 * 2 * D + D,
                           [[2 * D, 128], [L * 2 * D, 8], [1, D]]),
                ), deps=[t_ag] + gd_pump["kvr"] +
                        ([prev_pva[lt - 1]] if lt >= 1 else []))
                for p4 in range(4):
                    if p4 == 2:
                        kvds[1] = mk_kload(lt, 1, [prev_mul[lt]])
                    buf = tmpg[p4 % 2]
                    for j in range(2):
                        t = 2 * p4 + j
                        half, t4 = t // 4, t % 4
                        mdeps = [kvds[half], t_q]
                        if lt == 0 and p4 < 2 and j == 0:
                            mdeps += gd_pump[f"tmpg{p4 % 2}"]
                        if p4 >= 2 and j == 0:
                            mdeps.append(prev_red[(lt, p4 - 2)])
                        if lt >= 1 and p4 < 2 and j == 0:
                            mdeps.append(prev_red[(lt - 1, p4 + 2)])
                        prev_mul[lt] = S.op("ge", lambda e, lt=lt, t4=t4,
                                            j=j, buf=buf: e.tensor_mul(
                            buf[:, j * D:(j + 1) * D], qbp[lt % 2][:, :],
                            kvk[:, t4 * D:(t4 + 1) * D]), deps=mdeps)
                    rdeps = [prev_mul[lt]]
                    if lt < 2 and p4 == 0:
                        rdeps += gd_pump[f"scp{lt % 2}"]
                    if lt >= 2 and p4 == 0:
                        rdeps.append(prev_norm[lt - 2])
                    prev_red[(lt, p4)] = S.op("ve", lambda e, p4=p4, sc4=sc4, buf=buf:
                                              e.tensor_reduce(
                        AP(sc4, 2 * p4, [[128, 128], [1, 2], [8, 16]]),
                        AP(buf, 0, [[2 * D, 128], [D, 2], [64, 16], [1, 64]]),
                        AX.X, ALU.add), deps=rdeps)
                    yield
                c1 = S.op("ve", lambda e, sc4=sc4: e.tensor_add(
                    sc4[:, :], sc4[:, :], tmaskrep[:, :]),
                    deps=[prev_red[(lt, 3)], t_tm])
                yield
                c4 = S.op("se", lambda e, sc4=sc4: e.activation(
                    sc4[:, :], sc4[:, :], AF.Exp), deps=[c1])
                c5 = S.op("ve", lambda e, sc4=sc4: e.tensor_reduce(
                    smp[:, 0:16], AP(sc4, 0, [[128, 128], [8, 16], [1, 8]]),
                    AX.X, ALU.add), deps=[c4] + (gd_pump["smp"] if lt == 0 else [])
                    + ([prev_norm[lt - 1]] if lt >= 1 else []))
                c6 = S.op("ve", lambda e: e.reciprocal(smp[:, 16:32], smp[:, 0:16]),
                          deps=[c5])
                yield
                pv_add = None
                for t in range(8):
                    half, t4 = t // 4, t % 4
                    vdeps = [c4]
                    if lt == 0 and t < 2:
                        vdeps += gd_pump[f"tmpv{t % 2}"]
                    if t >= 2:
                        vdeps.append(prev_pva[lt])
                    pv_m = S.op("ve", lambda e, t=t, sc4=sc4: e.tensor_mul(
                        tmpv[t % 2][:, :],
                        kvr[:, t * D:(t + 1) * D],
                        AP(sc4, t, [[128, 128], [8, 16], [0, 64]])),
                        deps=vdeps + [t_vld])
                    adeps = [pv_m]
                    if t == 0:
                        adeps += (gd_pump[f"accp{lt % 2}"] if lt < 2 else
                                  [prev_acst[lt - 2]])
                    else:
                        adeps.append(prev_pva[lt])
                    pv_add = S.op("ge", lambda e, lt=lt, t=t: (
                        e.tensor_copy(accp[lt % 2][:, :], tmpv[t % 2][:, :]) if t == 0
                        else e.tensor_add(accp[lt % 2][:, :], accp[lt % 2][:, :],
                                          tmpv[t % 2][:, :])), deps=adeps)
                    prev_pva[lt] = pv_add
                    if t % 2 == 1:
                        yield
                prev_norm[lt] = S.op("ve", lambda e, lt=lt: e.tensor_tensor(
                    accp[lt % 2][:, :], accp[lt % 2][:, :],
                    AP(smp, 16, [[32, 128], [1, 16], [0, 64]]), ALU.mult),
                    deps=[pv_add, c6])
                yield
                acc_store[lt] = S.dma("st", lambda e, lt=lt: e.dma_start(
                    out=acc_sp[lt * 128:(lt + 1) * 128, :], in_=accp[lt % 2][:, :]),
                    deps=[prev_norm[lt]])
                prev_acst[lt] = acc_store[lt]
            while True:
                yield

        pump = pump_gen()
        pump_left = [NT * 11 + 5]

        def pump_step(n):
            for _ in range(n):
                if pump_left[0] > 0:
                    next(pump)
                    pump_left[0] -= 1




        P_sb = [al.alloc(f"P{i}", [128, 512], BF16) for i in range(4)]
        dnrb = al.alloc("dnrb", [128, 1024], BF16)

        gd_P = [al.guards[f"P{i}"] for i in range(4)]

        gd_dnrb = al.guards["dnrb"]
        last_exp = {}
        prev_yse = [None, None]
        prev_m12 = [None, None]
        prev_r = [None, None]
        prev_tt = [None, None]
        unit_no = [0]
        PUMP_START = 8
        for hp in range(8):
            for qb in range(4):
                ks = []
                for kb in range(4 * qb + 4):
                    j = kb - 4 * qb
                    ks.append((kb, 128 * j if j >= 0 else 0))
                nk = len(ks)
                exp_t = {}
                av_t = {}
                sc_t = {}
                for i, (kb, off) in enumerate(ks):
                    bnd = off > 0 or kb == 4 * qb
                    for par in range(2):
                        sbank = pb[2 + par]
                        deps = [q_evt[-1], k_evt[-1], t_trib, t_ib, t_vall]
                        if (0, par) in last_exp:
                            deps.append(last_exp[(0, par)])

                        def scfn(e, hp=hp, qb=qb, kb=kb, off=off, par=par,
                                 sbank=sbank, bnd=bnd):
                            ins = e.matmul(
                                sbank[:, off:512],
                                K_fm[par * 64:par * 64 + 64,
                                     hp * L + kb * 128: hp * L + (kb + 1) * 128],
                                Q_fm[par * 64:par * 64 + 64,
                                     hp * L + qb * 512 + off: hp * L + (qb + 1) * 512],
                                start=True, stop=not bnd)
                            if bnd:
                                ins = e.matmul(sbank[:, off:off + 128],
                                               ident_b[:, :], tri_b[:, :],
                                               start=False, stop=True)
                            return ins
                        sc_t[(i, par)] = S.op("te", scfn, deps=deps)
                    for par in range(2):
                        sbank = pb[2 + par]
                        deps = [sc_t[(i, par)]]
                        if i >= 2:
                            deps.append(av_t[(i - 2, par, "den")])
                        deps += gd_P[(i % 2) * 2 + par]
                        exp_t[(i, par)] = S.op("se", lambda e, off=off, sbank=sbank,
                                               par=par, i=i: e.activation(
                            P_sb[(i % 2) * 2 + par][:, off:512],
                            sbank[:, off:512], AF.Exp, bias=cshb[:, 0:1]),
                            deps=deps + [t_csh])
                        last_exp[(0, par)] = exp_t[(i, par)]
                    for par in range(2):
                        h = 2 * hp + par
                        ybank = pb[0] if par == 0 else pb[4]
                        dbank = pb[1] if par == 0 else pb[5]
                        deps = [exp_t[(i, par)], t_vall]
                        if i == 0 and prev_yse[par] is not None:
                            deps.append(prev_yse[par])
                            deps.append(prev_m12[par])
                            deps.append(prev_tt[par])
                        av_t[(i, par, "y")] = S.op("te", lambda e, kb=kb, off=off,
                                                   par=par, h=h, i=i, nk=nk,
                                                   ybank=ybank: e.matmul(
                            ybank[par * 64:(par + 1) * 64, off:512],
                            V_sb[:, kb * D + h * HD: kb * D + (h + 1) * HD],
                            P_sb[(i % 2) * 2 + par][:, off:512],
                            start=(i == 0), stop=(i == nk - 1),
                            tile_position=(0, par * 64)), deps=deps)
                        av_t[(i, par, "den")] = S.op("te", lambda e, off=off, par=par,
                                                     i=i, nk=nk, dbank=dbank: e.matmul(
                            dbank[0:1, off:512],
                            ones_b[:, 0:1],
                            P_sb[(i % 2) * 2 + par][:, off:512],
                            start=(i == 0), stop=(i == nk - 1)),
                            deps=[exp_t[(i, par)], t_ones_b, t_vall] +
                                 ([prev_r[par]] if (i == 0 and prev_r[par] is not None)
                                  else []))
                rr0 = S.op("ve", lambda e: e.reciprocal(
                    scr[0:1, 0:512], pb[1][0:1, 0:512]),
                    deps=[av_t[(nk - 1, 0, "den")]] +
                         ([prev_r[1]] if prev_r[1] is not None else [ip_evt[-1]]))
                prev_r[0] = rr0
                rr1 = S.op("ve", lambda e: e.reciprocal(
                    scr[0:1, 512:1024], pb[5][0:1, 0:512]),
                    deps=[av_t[(nk - 1, 1, "den")], rr0])
                rcast = S.op("ve", lambda e: e.tensor_copy(dnrb[0:1, :], scr[0:1, :]),
                             deps=[rr1] + gd_dnrb)
                prev_r[1] = rcast
                yse = [None, None]
                yse[0] = S.op("se", lambda e, hp=hp, qb=qb: e.activation(
                    y_fm[0:64, hp * L + qb * 512: hp * L + (qb + 1) * 512],
                    pb[0][0:64, 0:512], AF.Copy),
                    deps=[av_t[(nk - 1, 0, "y")]] + gd_yfm)
                yse[1] = S.op("se", lambda e, hp=hp, qb=qb: e.activation(
                    y_fm[64:128, hp * L + qb * 512: hp * L + (qb + 1) * 512],
                    pb[4][64:128, 0:512], AF.Copy),
                    deps=[av_t[(nk - 1, 1, "y")]] + gd_yfm)
                m1 = S.op("te", lambda e: e.matmul(
                    pb[0][0:64, 0:512], ones_b[0:1, 0:64], dnrb[0:1, 0:512],
                    start=True, stop=True),
                    deps=[rcast, yse[0]])
                m2 = S.op("te", lambda e: e.matmul(
                    pb[0][64:128, 0:512], ones_b[0:1, 0:64], dnrb[0:1, 512:1024],
                    start=True, stop=True, tile_position=(0, 64)),
                    deps=[m1] + ([prev_tt[1]] if prev_tt[1] is not None else []))
                prev_m12 = [m1, m2]
                prev_tt[0] = S.op("ve", lambda e, hp=hp, qb=qb: e.tensor_tensor(
                    y_fm[0:64, hp * L + qb * 512: hp * L + (qb + 1) * 512],
                    y_fm[0:64, hp * L + qb * 512: hp * L + (qb + 1) * 512],
                    pb[0][0:64, 0:512], ALU.mult), deps=[m1, yse[0]])
                prev_tt[1] = S.op("ve", lambda e, hp=hp, qb=qb: e.tensor_tensor(
                    y_fm[64:128, hp * L + qb * 512: hp * L + (qb + 1) * 512],
                    y_fm[64:128, hp * L + qb * 512: hp * L + (qb + 1) * 512],
                    pb[0][64:128, 0:512], ALU.mult), deps=[m2, yse[1], prev_tt[0]])
                prev_yse = yse
                unit_no[0] += 1
                if unit_no[0] > PUMP_START:
                    pump_step(7)
        prev_unit_yev = prev_tt[1]
        t_yfm = prev_unit_yev
        t_att_te = m2
        t_att_se = max(last_exp.values(), key=lambda t: t.val)
        al.drop("P0", "P1", "P2", "P3", after=[t_att_te])
        al.drop("dnrb", after=[t_att_te, t_yfm])

        al.drop("Q_fm", "K_fm", "V_sb", after=[t_att_te])
        dump("y_fm", y_fm)
        if upto == "ph2":
            return _finish(S, blk, dbg_marks)

        wout_sb = al.alloc("wout_sb", [128, DC * D], BF16)
        stile = [al.alloc(f"stile{i}", [128, D], F32) for i in range(2)]
        t_wout = wload(wout_sb, wout, D, D, deps=al.guards["wout_sb"])
        gd_st = [al.guards["stile0"], al.guards["stile1"]]
        s_store = {}

        def s_mms(e, u, ps):
            tb, half = u
            for dc in range(DC):
                ins = e.matmul(ps[:, 0:512],
                               y_fm[:, dc * L + tb * 128: dc * L + (tb + 1) * 128],
                               wout_sb[:, dc * D + half * 512: dc * D + (half + 1) * 512],
                               start=(dc == 0), stop=(dc == DC - 1))
            return ins

        def s_ev(e, u, ps):
            tb, half = u
            return e.activation(stile[tb % 2][:, half * 512:(half + 1) * 512],
                                ps[:, 0:512], AF.Copy)

        def s_evdeps(u):
            tb, half = u
            d = list(gd_st[tb % 2])
            if half == 0 and tb >= 2:
                d.append(s_store[tb - 2])
            return d

        s_mm = []
        s_evt = []
        for n, u in enumerate([(tb, half) for tb in range(NT) for half in range(2)]):
            tb, half = u
            ps = pb[n % 2]
            deps = [t_wout, t_yfm]
            if n >= 2:
                deps.append(s_evt[n - 2])
            s_mm.append(S.op("te", lambda e, u=u, ps=ps: s_mms(e, u, ps), deps=deps))
            pump_step(4)
            s_evt.append(S.op("se", lambda e, u=u, ps=ps: s_ev(e, u, ps),
                              deps=[s_mm[-1]] + s_evdeps(u)))
            if half == 1:
                s_store[tb] = S.dma("st", lambda e, tb=tb: e.dma_start(
                    out=s_sp[tb * 128:(tb + 1) * 128, :], in_=stile[tb % 2][:, :]),
                    deps=[s_evt[-1]])
        pump_step(NT * 11 + 5)
        al.drop("y_fm", "wout_sb", after=[s_mm[-1]])
        al.drop("stile0", "stile1", after=[s_store[14], s_store[15]])
        if upto == "ph3":
            return _finish(S, blk, dbg_marks,
                           extra=[s_store[tb] for tb in range(NT)])

        al.drop("kvr", "kvk", "qbp0", "qbp1", "tmpg0", "tmpg1", "scp0", "scp1",
                "smp", "tmpv0", "tmpv1", "accp0", "accp1",
                after=[acc_store[NT - 2], acc_store[NT - 1],
                       list(acc_store.values())[-1]])
        accl = [al.alloc(f"accl{i}", [128, D], F32) for i in range(2)]
        xb4 = [al.alloc(f"xb4{i}", [128, D], F32) for i in range(2)]
        wop_sb = al.alloc("wop_sb", [128, DC * D], BF16)
        opb_rep = al.alloc("opb_rep", [128, D], F32)
        accfm = al.alloc("accfm", [128, D], BF16)
        tmpf4 = al.alloc("tmpf4", [128, D], F32)
        x1t = [al.alloc(f"x1t{i}", [128, D], F32) for i in range(2)]
        t_wop = wload(wop_sb, wopT, D, D, deps=al.guards["wop_sb"])
        t_opb = S.dma("ld2", lambda e: e.dma_start(
            out=opb_rep[:, :], in_=AP(opb.tensor, 0, [[0, 128], [1, D]])),
            deps=al.guards["opb_rep"])
        gd4 = {n: al.guards[n] for n in ("accl0", "accl1", "xb40", "xb41",
                                         "accfm", "tmpf4", "x1t0", "x1t1")}
        x1_store = {}
        last_x1 = {}
        prev_mm_halves = [t_att_se, t_yfm]
        prev_ev_last = None
        for lt in range(NT):
            t_al = S.dma("ld", lambda e, lt=lt: e.dma_start(
                out=accl[lt % 2][:, :], in_=acc_sp[lt * 128:(lt + 1) * 128, :]),
                deps=[acc_store[lt]] + gd4[f"accl{lt % 2}"] +
                     ([last_x1[lt - 2]] if lt >= 2 else []))
            t_x = S.dma("ld2", lambda e, lt=lt: e.dma_start(
                out=xb4[lt % 2][:, :], in_=x[lt * 128:(lt + 1) * 128, :]),
                deps=gd4[f"xb4{lt % 2}"] +
                     ([last_x1[lt - 2]] if lt >= 2 else []))
            tps = []
            evs = []
            for dc in range(DC):
                tpd = [t_al, t_if] + prev_mm_halves
                if dc >= 2:
                    tpd.append(evs[dc - 2])
                tps.append(S.op("te", lambda e, dc=dc, lt=lt: e.transpose(
                    pb[2 + (dc % 2)][:, 0:128], accl[lt % 2][:, dc * 128:(dc + 1) * 128],
                    ident_f[:, :]), deps=tpd))
                evs.append(S.op("se", lambda e, dc=dc: e.activation(
                    accfm[:, dc * 128:(dc + 1) * 128], pb[2 + (dc % 2)][:, 0:128],
                    AF.Copy), deps=[tps[dc]] +
                    (gd4["accfm"] if (lt == 0 and dc == 0) else
                     ([prev_ev_last] if (dc == 0 and prev_ev_last is not None) else []))))
            prev_ev_last = evs[-1]
            mm_halves = []
            for half in range(2):
                def opmm(e, half=half, lt=lt):
                    for dc2 in range(DC):
                        ins = e.matmul(
                            pb[4 + half][:, 0:512],
                            accfm[:, dc2 * 128:(dc2 + 1) * 128],
                            wop_sb[:, dc2 * D + half * 512: dc2 * D + (half + 1) * 512],
                            start=(dc2 == 0), stop=(dc2 == DC - 1))
                    return ins
                mm_halves.append(S.op("te", opmm,
                                      deps=[evs[-1], t_wop] +
                                      ([last_x1[lt - 1]] if lt >= 1
                                       else [t_att_se, t_yfm])))
            prev_mm_halves = mm_halves
            xn1 = S.op("ve", lambda e, lt=lt: e.tensor_scalar(
                tmpf4[:, :], xb4[lt % 2][:, :], rstd0[:, lt:lt + 1],
                nmr0[:, lt:lt + 1], ALU.mult, ALU.add),
                deps=[t_x] + (gd4["tmpf4"] if lt == 0 else [last_x1[lt - 1]]))
            hdeps = []
            for half in range(2):
                hdeps.append(S.op("ve", lambda e, lt=lt, half=half: e.tensor_add(
                    x1t[lt % 2][:, half * 512:(half + 1) * 512],
                    pb[4 + half][:, 0:512], tmpf4[:, half * 512:(half + 1) * 512]),
                    deps=[xn1, mm_halves[half]] +
                    (gd4[f"x1t{lt % 2}"] if lt < 2 else [x1_store[lt - 2]])))
            t_x1 = S.op("ve", lambda e, lt=lt: e.tensor_add(
                x1t[lt % 2][:, :], x1t[lt % 2][:, :], opb_rep[:, :]),
                deps=hdeps + [t_opb])
            last_x1[lt] = t_x1
            x1_store[lt] = S.dma("st", lambda e, lt=lt: e.dma_start(
                out=x1_sp[lt * 128:(lt + 1) * 128, :], in_=x1t[lt % 2][:, :]),
                deps=[t_x1])
        ph4_guard = [last_x1[NT - 1], x1_store[NT - 2], x1_store[NT - 1],
                     prev_mm_halves[-1], evs[-1]]
        al.drop("accl0", "accl1", "xb40", "xb41", "wop_sb", "opb_rep",
                "accfm", "tmpf4", "x1t0", "x1t1", after=ph4_guard)

        if upto == "ph4":
            if "agio" in dbg:
                dai = nc.dram_tensor("dbg_agin", [L, 2 * D], BF16, kind="ExternalOutput").ap()
                dao = nc.dram_tensor("dbg_agout", [8 * L, 2 * D], BF16, kind="ExternalOutput").ap()
                S.dma("st", lambda e: e.dma_start(out=dai, in_=ag_in),
                      deps=[kv_store[tb] for tb in range(NT)])
                S.dma("st", lambda e: e.dma_start(out=dao, in_=ag_out), deps=[t_ag])
            if "x1" in dbg:
                dx = nc.dram_tensor("dbg_x1", [L, D], F32, kind="ExternalOutput").ap()
                S.dma("st", lambda e: e.dma_start(out=dx, in_=x1_sp),
                      deps=[x1_store[lt] for lt in range(NT)])
            return _finish(S, blk, dbg_marks,
                           extra=[x1_store[lt] for lt in range(NT)])

        x1_tok = al.alloc("x1_tok", [128, NT * D], F32)
        g2 = al.alloc("g2rep", [128, D], F32)
        b2 = al.alloc("b2rep", [128, D], F32)
        hn2_tok = al.alloc("hn2_tok", [128, NT * D], BF16)
        t_g2 = S.dma("ld2", lambda e: e.dma_start(
            out=g2[:, :], in_=AP(ln2w.tensor, 0, [[0, 128], [1, D]])),
            deps=al.guards["g2rep"])
        t_b2 = S.dma("ld2", lambda e: e.dma_start(
            out=b2[:, :], in_=AP(ln2b.tensor, 0, [[0, 128], [1, D]])),
            deps=al.guards["b2rep"])
        x1ld = []
        for h in range(2):
            x1ld.append(S.dma("ld", lambda e, h=h: e.dma_start(
                out=AP(x1_tok, h * 8 * D, [[NT * D, 128], [D, 8], [1, D]]),
                in_=AP(x1_sp.tensor, h * 8 * 128 * D, [[D, 128], [128 * D, 8], [1, D]])),
                deps=[x1_store[h * 8 + i] for i in range(8)] + al.guards["x1_tok"]))
        hn2_outs, _, ln2_t5 = ln_pass(
            "ln2", x1_tok, rstd2, nmr2,
            [[x1ld[t // 8], t_g2, t_b2] for t in range(NT)],
            normalize=False, gamma=g2, beta=b2, out_tok=hn2_tok,
            out_guards=al.guards["hn2_tok"])
        al.drop("g2rep", "b2rep", after=[hn2_outs[-1]])
        al.drop("x1_tok", after=[hn2_outs[-1], ln2_t5])
        hn2_fm = al.alloc("hn2_fm", [128, DC * L], BF16)
        gd_h2 = al.guards["hn2_fm"]
        h2ev = {}
        h2tp = {}
        for t in range(NT):
            for dc in range(DC):
                n = t * DC + dc
                deps = [hn2_outs[t], t_ib, t_att_se]
                if n >= 2:
                    deps.append(h2ev[n - 2])
                h2tp[n] = S.op("te", lambda e, t=t, dc=dc, n=n: e.transpose(
                    pbT[n % 2][:, 0:128],
                    hn2_tok[:, t * D + dc * 128: t * D + (dc + 1) * 128],
                    ident_b[:, :]), deps=deps)
                h2ev[n] = S.op("ve", lambda e, t=t, dc=dc, n=n: e.tensor_copy(
                    hn2_fm[:, dc * L + t * 128: dc * L + (t + 1) * 128],
                    pbT[n % 2][:, 0:128]),
                    deps=[h2tp[n]] + gd_h2)
        t_h2fm = h2ev[NT * DC - 1]
        al.drop("hn2_tok", after=[h2tp[NT * DC - 1]])
        dump("hn2_fm", hn2_fm)

        wr_sb = al.alloc("wr_sb", [128, DC * E], BF16)
        rlog = al.alloc("rlog", [128, NT * E], F32)
        rtmp = al.alloc("rtmp", [128, NT * E], F32)
        rcol = al.alloc("rcol", [128, 2 * NT], F32)
        t_wr = wload(wr_sb, wrT, D, E, deps=al.guards["wr_sb"])
        gd_rl = al.guards["rlog"] + al.guards["rtmp"] + al.guards["rcol"]
        rcop = {}
        prev_mm = None
        for tb in range(NT):
            bidx = 4 + tb % 2
            def rmm(e, tb=tb, bidx=bidx):
                for dc in range(DC):
                    ins = e.matmul(pb[bidx][:, 0:E],
                                   hn2_fm[:, dc * L + tb * 128: dc * L + (tb + 1) * 128],
                                   wr_sb[:, dc * E:(dc + 1) * E],
                                   start=(dc == 0), stop=(dc == DC - 1))
                return ins
            mm = S.op("te", rmm, deps=[t_wr, t_h2fm] +
                      ([rcop[tb - 2]] if tb >= 2 else []))
            rcop[tb] = S.op("se", lambda e, tb=tb, bidx=bidx: e.activation(
                rlog[:, tb * E:(tb + 1) * E], pb[bidx][:, 0:E], AF.Copy),
                deps=[mm] + (gd_rl if tb == 0 else []))
            prev_mm = mm
        e1 = S.op("se", lambda e: e.activation(rlog[:, :], rlog[:, :], AF.Exp),
                  deps=[rcop[NT - 1]])
        v1 = S.op("ve", lambda e: e.tensor_reduce(
            rcol[:, 0:NT], AP(rlog, 0, [[NT * E, 128], [E, NT], [1, E]]),
            AX.X, ALU.add), deps=[e1])
        v2 = S.op("ve", lambda e: e.reciprocal(rcol[:, 0:NT], rcol[:, 0:NT]), deps=[v1])
        v3 = S.op("ve", lambda e: e.tensor_tensor(
            rlog[:, :], rlog[:, :],
            AP(rcol, 0, [[2 * NT, 128], [1, NT], [0, E]]), ALU.mult), deps=[v2])
        v4 = S.op("ve", lambda e: e.tensor_reduce(
            rcol[:, NT:2 * NT], AP(rlog, 0, [[NT * E, 128], [E, NT], [1, E]]),
            AX.X, ALU.max), deps=[v3])
        v5 = S.op("ve", lambda e: e.tensor_tensor(
            rtmp[:, :], rlog[:, :],
            AP(rcol, NT, [[2 * NT, 128], [1, NT], [0, E]]), ALU.is_ge), deps=[v4])
        v6a = S.op("ve", lambda e: e.tensor_scalar(
            rtmp[:, :], rtmp[:, :], -1.0, 1.0, ALU.mult, ALU.add), deps=[v5])
        v6b = S.op("ve", lambda e: e.tensor_mul(rtmp[:, :], rtmp[:, :], rlog[:, :]),
                   deps=[v6a])
        v7 = S.op("ve", lambda e: e.tensor_reduce(
            rcol[:, NT:2 * NT], AP(rtmp, 0, [[NT * E, 128], [E, NT], [1, E]]),
            AX.X, ALU.max), deps=[v6b])
        v8 = S.op("ve", lambda e: e.tensor_tensor(
            rtmp[:, :], rlog[:, :],
            AP(rcol, NT, [[2 * NT, 128], [1, NT], [0, E]]), ALU.is_ge), deps=[v7])
        t_gates = S.op("ve", lambda e: e.tensor_mul(gates[:, :], rtmp[:, :],
                                                    rlog[:, :]), deps=[v8])
        if upto == "ph5":
            return _finish(S, blk, dbg_marks)

        w1_sb = al.alloc("w1_sb", [128, DC * F], BF16)
        w2_sb = al.alloc("w2_sb", [128, FC * D], BF16)
        hmid = [al.alloc(f"hmid{i}", [128, FC * 512], BF16) for i in range(2)]
        y_acc2 = [al.alloc(f"y_acc{i}", [128, 8 * D], F32) for i in range(2)]
        gd_ya = al.guards["y_acc0"] + al.guards["y_acc1"]
        ya = lambda tb: y_acc2[tb // 8][:, (tb % 8) * D:(tb % 8 + 1) * D]
        gd_hm = [al.guards["hmid0"], al.guards["hmid1"]]
        w1ld = {}
        w2ld = {}
        w1_lastmm = {}
        w2_lastmm = {}
        yev = {}

        def w1load(ee):
            deps = list(al.guards["w1_sb"]) if ee == 0 else [w1_lastmm[ee - 1]]
            return S.dma("wt", lambda e, ee=ee: e.dma_start(
                out=AP(w1_sb, 0, [[DC * F, 128], [F, DC], [1, F]]),
                in_=AP(w1.tensor, ee * D * F, [[F, 128], [128 * F, DC], [1, F]])),
                deps=deps)

        def w2load(ee):
            deps = list(al.guards["w2_sb"]) if ee == 0 else [w2_lastmm[ee - 1]]
            return S.dma("wt", lambda e, ee=ee: e.dma_start(
                out=AP(w2_sb, 0, [[FC * D, 128], [D, FC], [1, D]]),
                in_=AP(w2.tensor, ee * F * D, [[D, 128], [128 * D, FC], [1, D]])),
                deps=deps)

        w1ld[0] = w1load(0)
        w2ld[0] = w2load(0)
        prev_yev_half = {}
        last_w2mm = {}
        prev_w1ev = {}
        prev_gelu = {}
        prev_w1ev = {}
        prev_yev = [t_att_se, None]
        last_gelu_of_tt = {}
        prev_w1_lastev = None
        prev_any_yev = None
        for ee in range(E):
            if ee + 1 < E:
                w1ld[ee + 1] = None
            order = []
            for tt in range(4):
                order.append(("w1", tt))
                if tt >= 1:
                    order.append(("w2", tt - 1))
            order.append(("w2", 3))
            for kind, tt in order:
                if kind == "w1":
                    for fc in range(FC):
                        bidx = 2 + (fc % 2)
                        deps = [w1ld[ee],
                                h2ev[(4 * tt + 4) * DC - 1] if ee == 0 else t_h2fm]
                        key = (fc % 2,)
                        if key in prev_w1ev:
                            deps.append(prev_w1ev[key])
                        def w1mm(e, tt=tt, fc=fc, bidx=bidx):
                            for hc in range(DC):
                                ins = e.matmul(
                                    pb[bidx][:, 0:512],
                                    w1_sb[:, hc * F + fc * 128: hc * F + (fc + 1) * 128],
                                    hn2_fm[:, hc * L + tt * 512: hc * L + (tt + 1) * 512],
                                    start=(hc == 0), stop=(hc == DC - 1))
                            return ins
                        mmt = S.op("te", w1mm, deps=deps)
                        w1_lastmm[ee] = mmt
                        gdeps = [mmt]
                        if (tt % 2, fc) in prev_gelu:
                            gdeps.append(prev_gelu[(tt % 2, fc)])
                        if (tt % 2) in last_w2mm:
                            gdeps.append(last_w2mm[tt % 2])
                        if ee == 0 and tt < 2:
                            gdeps += gd_hm[tt % 2]
                        gt = S.op("se", lambda e, tt=tt, fc=fc, bidx=bidx: e.activation(
                            hmid[tt % 2][:, fc * 512:(fc + 1) * 512],
                            pb[bidx][:, 0:512], AF.Gelu), deps=gdeps)
                        prev_gelu[(tt % 2, fc)] = gt
                        prev_w1ev[(fc % 2,)] = gt
                        last_gelu_of_tt[tt] = gt
                else:
                    for tb in range(4 * tt, 4 * tt + 4):
                        for half in range(2):
                            bidx = 4 + half
                            deps = [w2ld[ee], last_gelu_of_tt[tt]]
                            if half in prev_yev_half:
                                deps.append(prev_yev_half[half])
                            def w2mm(e, tt=tt, tb=tb, half=half, bidx=bidx):
                                for fc in range(FC):
                                    ins = e.matmul(
                                        pb[bidx][:, 0:512],
                                        hmid[tt % 2][:, fc * 512 + (tb - 4 * tt) * 128:
                                                     fc * 512 + (tb - 4 * tt) * 128 + 128],
                                        w2_sb[:, fc * D + half * 512: fc * D + (half + 1) * 512],
                                        start=(fc == 0), stop=(fc == FC - 1))
                                return ins
                            mmt = S.op("te", w2mm, deps=deps)
                            w2_lastmm[ee] = mmt
                            last_w2mm[tt % 2] = mmt
                            ydeps = [mmt, t_gates]
                            if ee == 0:
                                ydeps += gd_ya
                            gap = gates[:, tb * E + ee: tb * E + ee + 1]
                            dst = ya(tb)[:, half * 512:(half + 1) * 512]
                            if ee == 0:
                                yv = S.op("ve", lambda e, dst=dst, gap=gap, bidx=bidx:
                                          e.tensor_scalar_mul(dst, pb[bidx][:, 0:512], gap),
                                          deps=ydeps)
                            else:
                                yv = S.op("ve", lambda e, dst=dst, gap=gap, bidx=bidx:
                                          e.scalar_tensor_tensor(
                                              dst, pb[bidx][:, 0:512], gap, dst,
                                              ALU.mult, ALU.add), deps=ydeps)
                            yev[(tb, half)] = yv
                            prev_yev_half[half] = yv
            if ee + 1 < E:
                w1ld[ee + 1] = w1load(ee + 1)
                w2ld[ee + 1] = w2load(ee + 1)
        t_yacc = yev[(NT - 1, 1)]
        al.drop("w1_sb", after=[w1_lastmm[E - 1]])
        al.drop("w2_sb", after=[w2_lastmm[E - 1]])
        al.drop("hmid0", "hmid1", after=[w2_lastmm[E - 1]])
        al.drop("hn2_fm", after=[w1_lastmm[E - 1]])

        x1b = [al.alloc(f"x1b{i}", [128, D], F32) for i in range(2)]
        sb7 = [al.alloc(f"sb7{i}", [128, D], F32) for i in range(2)]
        xb7 = [al.alloc(f"xb7{i}", [128, D], F32) for i in range(2)]
        ot7 = [al.alloc(f"ot7{i}", [128, D], F32) for i in range(2)]
        gd7 = {n: al.guards[n] for n in ("x1b0", "x1b1", "sb70", "sb71",
                                         "xb70", "xb71", "ot70", "ot71")}
        out_st = {}
        prev_ops = {}
        for tb in range(NT):
            war = [prev_ops[tb - 2]] if tb >= 2 else []
            t_l1 = S.dma("ld", lambda e, tb=tb: e.dma_start(
                out=x1b[tb % 2][:, :], in_=x1_sp[tb * 128:(tb + 1) * 128, :]),
                deps=[x1_store[tb]] + gd7[f"x1b{tb % 2}"] + war)
            t_ls = S.dma("kv", lambda e, tb=tb: e.dma_start(
                out=sb7[tb % 2][:, :], in_=s_sp[tb * 128:(tb + 1) * 128, :]),
                deps=[s_store[tb]] + gd7[f"sb7{tb % 2}"] + war)
            t_lx = S.dma("ld2", lambda e, tb=tb: e.dma_start(
                out=xb7[tb % 2][:, :], in_=x[tb * 128:(tb + 1) * 128, :]),
                deps=gd7[f"xb7{tb % 2}"] + war)
            f1 = S.op("ve", lambda e, tb=tb: e.tensor_add(
                x1b[tb % 2][:, :], x1b[tb % 2][:, :], ya(tb)),
                deps=[t_l1, yev[(tb, 0)], yev[(tb, 1)]])
            f2 = S.op("se", lambda e, tb=tb: e.activation(
                x1b[tb % 2][:, :], x1b[tb % 2][:, :], AF.Silu), deps=[f1])
            f3 = S.op("ve", lambda e, tb=tb: e.tensor_mul(
                x1b[tb % 2][:, :], x1b[tb % 2][:, :], sb7[tb % 2][:, :]),
                deps=[f2, t_ls])
            f4 = S.op("ve", lambda e, tb=tb: e.tensor_add(
                ot7[tb % 2][:, :], x1b[tb % 2][:, :], xb7[tb % 2][:, :]),
                deps=[f3, t_lx] + gd7[f"ot7{tb % 2}"] +
                     ([out_st[tb - 2]] if tb >= 2 else []))
            prev_ops[tb] = f4
            out_st[tb] = S.dma("st", lambda e, tb=tb: e.dma_start(
                out=out[tb * 128:(tb + 1) * 128, :], in_=ot7[tb % 2][:, :]),
                deps=[f4])
        return _finish(S, blk, dbg_marks, extra=[out_st[tb] for tb in range(NT)])


def _finish(S, blk, dbg_marks, extra=()):
    deps = [Tok(S.sem[e], S.cnt[e]) for e in ("te", "ve", "se", "ge") if S.cnt[e]]
    deps += [d for d in extra if d is not None]
    for name, (t, dram) in dbg_marks.items():
        S.dma("st", lambda e, t=t, dram=dram: e.dma_start(out=dram, in_=t[:, :]),
              deps=deps)

    def fin(e):
        for fam in ("st", "kv", "ld", "ld2", "wt"):
            q, sems, cnts = S.fam[fam]
            for sm, c in zip(sems, cnts):
                if c:
                    e.wait_ge(sm, 16 * c)
        if S.cc_cnt:
            e.wait_ge(S.cc_sem, S.cc_cnt)
        return e.nop()

    S.op("sy", fin, deps=deps, inc=False)
    S.emit(blk)
    return S.nc


def prep_inputs(inputs):
    g = lambda k: np.ascontiguousarray(np.asarray(inputs[k], dtype=np.float32))
    x = g("L0_attn_x_bld")
    shared = dict(
        wq=g("L0_attn_Wq").reshape(D, D),
        wkv=g("L0_attn_Wkv_down"),
        wkup=g("L0_attn_Wk_up").reshape(CL, D),
        wvup=g("L0_attn_Wv_up").reshape(CL, D),
        wout=g("L0_attn_Wout").reshape(D, D),
        wipT=np.ascontiguousarray(g("L0_ffn_in_proj_w").T),
        wopT=np.ascontiguousarray(g("L0_ffn_out_proj_w").T),
        wrT=np.ascontiguousarray(g("L0_ffn_router_w").T),
        w1=g("L0_ffn_w1"), w2=g("L0_ffn_w2"),
        ln1w=g("L0_ffn_ln1_w"), ln1b=g("L0_ffn_ln1_b"),
        ln2w=g("L0_ffn_ln2_w"), ln2b=g("L0_ffn_ln2_b"),
        ipb=g("L0_ffn_in_proj_b"), opb=g("L0_ffn_out_proj_b"))
    in_maps = []
    for c in range(8):
        m = dict(shared)
        m["x"] = np.ascontiguousarray(x[c])
        m["tmask"] = np.where(np.arange(E) <= c, 0.0, NEG).astype(np.float32)
        in_maps.append(m)
    return in_maps


def kernel(**inputs):
    in_maps = prep_inputs(inputs)
    nc = build_program("full")
    res = run_bass_kernel_spmd(nc, in_maps, core_ids=list(range(8)))
    return np.stack([np.asarray(r["out"]) for r in res.results])


# version 24
# speedup vs baseline: 1.0198x; 1.0198x over previous
import contextlib
import numpy as np
import concourse.bass as bass
import concourse.mybir as mybir
from concourse.bass_utils import run_bass_kernel_spmd

F32 = mybir.dt.float32
BF16 = mybir.dt.bfloat16
AF = mybir.ActivationFunctionType
ALU = mybir.AluOpType
AX = mybir.AxisListType
AP = bass.AP

L, D = 2048, 1024
NT, DC = 16, 8
H, HD = 16, 64
CL, CLC = 256, 2
F, FC = 2048, 16
E = 8
EPS = 1e-5
NEG = -1e30
CSH = 44.0
SBUF_BASE = 16512
SBUF_TOP = 229344

ENGS = ("te", "ve", "se", "ge", "sy")


class Tok:
    __slots__ = ("sem", "val")

    def __init__(self, sem, val):
        self.sem, self.val = sem, val


class Sched:

    def __init__(self, nc, stack):
        self.nc = nc
        self.ops = {e: [] for e in ENGS}
        self.sem = {}
        for e in ("te", "ve", "se", "ge"):
            self.sem[e] = stack.enter_context(nc.semaphore("s_" + e))
        self.cnt = {e: 0 for e in ("te", "ve", "se", "ge")}
        self.fam = {}
        for name, q in (("ld", "sy"), ("ld2", "sy"), ("st", "sy"), ("kv", "sy"),
                        ("wt", "ge")):
            sems = [stack.enter_context(nc.semaphore(f"{name}{i}")) for i in range(2)]
            self.fam[name] = (q, sems, [0, 0])
        self.cc_sem = stack.enter_context(nc.semaphore("scc"))
        self.cc_cnt = 0

    def op(self, eng, fn, deps=(), inc=True):
        tok = None
        if inc:
            self.cnt[eng] += 1
            tok = Tok(self.sem[eng], self.cnt[eng])
        self.ops[eng].append((fn, list(deps), tok))
        return tok

    def dma(self, fam, fn, deps=()):
        q, sems, cnts = self.fam[fam]
        i = (cnts[0] + cnts[1]) % 2
        deps = list(deps)
        if cnts[i]:
            deps.append(Tok(sems[i], 16 * cnts[i]))
        cnts[i] += 1
        tok = Tok(sems[i], 16 * cnts[i])
        self.ops[q].append((fn, deps, Tok(sems[i], None)))
        return tok

    def collective(self, fn, deps=()):
        self.cc_cnt += 1
        tok = Tok(self.cc_sem, self.cc_cnt)
        self.ops["ge"].append((fn, list(deps), Tok(self.cc_sem, None)))
        return tok

    def emit(self, blk):
        bodies = {"te": blk.tensor, "ve": blk.vector, "se": blk.scalar,
                  "ge": blk.gpsimd, "sy": blk.sync}
        for eng in ENGS:
            def body(e, eng=eng):
                hw = {}
                for fn, deps, tok in self.ops[eng]:
                    for d in deps:
                        if d is None:
                            continue
                        key = d.sem.name
                        if hw.get(key, 0) >= d.val:
                            continue
                        e.wait_ge(d.sem, d.val)
                        hw[key] = d.val
                    ins = fn(e)
                    if tok is not None:
                        if tok.val is None:
                            amt = 16 if tok.sem is not self.cc_sem else 1
                            ins.then_inc(tok.sem, amt)
                        else:
                            ins.then_inc(tok.sem, 1)
            bodies[eng](body)


class Alloc:

    def __init__(self, nc):
        self.nc = nc
        self.free = [[SBUF_BASE, SBUF_TOP - SBUF_BASE, []]]
        self.live = {}
        self.guards = {}

    def alloc(self, name, shape, dtype):
        size = int(np.prod(shape[1:])) * (4 if dtype == F32 else 2)
        size = (size + 63) & ~63
        for i, (off, flen, g) in enumerate(self.free):
            if flen >= size:
                if flen == size:
                    self.free.pop(i)
                else:
                    self.free[i] = [off + size, flen - size, list(g)]
                t = self.nc.alloc_sbuf_tensor_at(name, shape, dtype, offset=off)
                self.live[name] = (off, size)
                self.guards[name] = list(g)
                return t
        raise MemoryError(f"SBUF alloc fail {name} {size}B free={self.free}")

    def drop(self, *names, after=()):
        after = [a for a in (after if isinstance(after, (list, tuple)) else [after])
                 if a is not None]
        for n in names:
            off, size = self.live.pop(n)
            self.free.append([off, size, list(after)])
            self.guards.pop(n, None)
        self.free.sort()
        out = []
        for off, ln, g in self.free:
            if out and out[-1][0] + out[-1][1] == off:
                out[-1][1] += ln
                out[-1][2] = out[-1][2] + g
            else:
                out.append([off, ln, list(g)])
        self.free = out


def build_program(upto="full", dbg=()):
    nc = bass.Bass("TRN2", target_bir_lowering=False)
    stack = contextlib.ExitStack()
    al = Alloc(nc)
    dbg = list(dbg)

    di = lambda n, s: nc.dram_tensor(n, s, F32, kind="ExternalInput").ap()
    x = di("x", [L, D])
    wq = di("wq", [D, D])
    wkv = di("wkv", [D, CL])
    wkup = di("wkup", [CL, D])
    wvup = di("wvup", [CL, D])
    wout = di("wout", [D, D])
    wipT = di("wipT", [D, 3 * D])
    wopT = di("wopT", [D, D])
    wrT = di("wrT", [D, E])
    w1 = di("w1", [E, D, F])
    w2 = di("w2", [E, F, D])
    ln1w = di("ln1w", [D])
    ln1b = di("ln1b", [D])
    ln2w = di("ln2w", [D])
    ln2b = di("ln2b", [D])
    ipb = di("ipb", [3 * D])
    opb = di("opb", [D])
    tmask = di("tmask", [E])
    out = nc.dram_tensor("out", [L, D], F32, kind="ExternalOutput").ap()
    ag_in = nc.dram_tensor("ag_in", [L, 2 * D], BF16).ap()
    ag_out = nc.dram_tensor("ag_out", [8 * L, 2 * D], BF16).ap()
    x1_sp = nc.dram_tensor("x1_sp", [L, D], F32).ap()
    q_sp = nc.dram_tensor("q_sp", [L, D], BF16).ap()
    acc_sp = nc.dram_tensor("acc_sp", [L, D], F32).ap()
    s_sp = nc.dram_tensor("s_sp", [L, D], F32).ap()

    pb = [nc.alloc_psum_tensor(f"pb{i}", [128, 512], F32) for i in range(6)]
    pbT = [nc.alloc_psum_tensor(f"pbT{i}", [128, 512], BF16) for i in range(2)]

    with stack:
        S = Sched(nc, stack)
        blk = stack.enter_context(nc.Block())

        stats = al.alloc("stats", [128, 32], F32)
        rstd0 = al.alloc("rstd0", [128, NT], F32)
        nmr0 = al.alloc("nmr0", [128, NT], F32)
        ident_f = al.alloc("ident_f", [128, 128], F32)
        ident_b = al.alloc("ident_b", [128, 128], BF16)

        tmaskrep = al.alloc("tmaskrep", [128, 128], F32)
        ones_b = al.alloc("ones_b", [128, 128], BF16)
        tri_b = al.alloc("tri_b", [128, 128], BF16)
        ones_f = al.alloc("ones_f", [128, 128], F32)
        gates = al.alloc("gates", [128, NT * E], F32)
        scr = al.alloc("scr", [128, D], F32)
        scr2 = al.alloc("scr2", [128, D], F32)
        dumacc = al.alloc("dumacc", [128, 1], F32)
        cshb = al.alloc("cshb", [128, 1], F32)
        rstd1 = al.alloc("rstd1", [128, NT], F32)
        nmr1 = al.alloc("nmr1", [128, NT], F32)
        rstd2 = al.alloc("rstd2", [128, NT], F32)
        nmr2 = al.alloc("nmr2", [128, NT], F32)

        dbg_marks = {}

        def dump(name, t):
            if name in dbg:
                dbg_marks[name] = (t, nc.dram_tensor(
                    "dbg_" + name, [128, int(t.shape[1])],
                    t.dtype, kind="ExternalOutput").ap())

        t_ones_b = S.op("ge", lambda e: e.memset(ones_b[:, :], 1.0))
        t_ones_f = S.op("ge", lambda e: e.memset(ones_f[:, :], 1.0))

        t_ib = S.op("ge", lambda e: e.affine_select(
            ident_b[:, :], ones_b[:, :], [[1, 128]], ALU.is_equal, 0.0,
            base=0, channel_multiplier=-1), deps=[t_ones_b])
        t_if = S.op("ge", lambda e: e.affine_select(
            ident_f[:, :], ones_f[:, :], [[1, 128]], ALU.is_equal, 0.0,
            base=0, channel_multiplier=-1), deps=[t_ones_f])
        al.drop("ones_f", after=[t_if])
        t_zb = S.op("ge", lambda e: e.memset(tri_b[:, :], 0.0))
        t_trib = S.op("ge", lambda e: e.affine_select(
            tri_b[:, :], tri_b[:, :], [[1, 128]], ALU.is_ge, NEG,
            base=0, channel_multiplier=-1), deps=[t_zb])
        t_tm = S.dma("ld2", lambda e: e.dma_start(
            out=tmaskrep[:, :], in_=AP(tmask.tensor, 0, [[0, 128], [0, 16], [1, 8]])))
        t_csh = S.op("ge", lambda e: e.memset(cshb[:, :], -CSH))

        def ln_pass(tag, src_tok, rstd, nmr, tile_deps, normalize=False,
                    gamma=None, beta=None, out_tok=None, out_guards=()):
            sq_toks = []
            prev = []
            for t in range(NT):
                tk = S.op("se", lambda e, t=t: e.activation(
                    scr[:, :], src_tok[:, t * D:(t + 1) * D], AF.Square,
                    accum_out=stats[:, 16 + t:17 + t]),
                    deps=list(tile_deps[t]) + prev)
                prev = [tk]
                sq_toks.append(tk)
            rd_toks = []
            for t in range(NT):
                rd_toks.append(S.op("ve", lambda e, t=t: e.reduce_sum(
                    stats[:, t:t + 1], src_tok[:, t * D:(t + 1) * D], axis=AX.X),
                    deps=tile_deps[t]))
            sums = stats[:, 0:16]
            sumsq = stats[:, 16:32]
            t1 = S.op("ve", lambda e: e.tensor_scalar_mul(sums[:, :], sums[:, :], 1.0 / D),
                      deps=[rd_toks[-1]])
            t2 = S.op("ve", lambda e: e.tensor_scalar(
                sumsq[:, :], sumsq[:, :], 1.0 / D, EPS, ALU.mult, ALU.add),
                deps=[sq_toks[-1], t1])
            t3 = S.op("ve", lambda e: e.tensor_mul(rstd[:, :], sums[:, :], sums[:, :]),
                      deps=[t1])
            t4 = S.op("ve", lambda e: e.tensor_sub(sumsq[:, :], sumsq[:, :], rstd[:, :]),
                      deps=[t2, t3])
            t5 = S.op("se", lambda e: e.activation(scr[:, 0:16], sumsq[:, :], AF.Sqrt),
                      deps=[t4, sq_toks[-1]])
            t6 = S.op("ve", lambda e: e.reciprocal(rstd[:, :], scr[:, 0:16]), deps=[t5])
            t7 = S.op("ve", lambda e: e.scalar_tensor_tensor(
                nmr[:, :], sums[:, :], -1.0, rstd[:, :], ALU.mult, ALU.mult), deps=[t6])
            outs = []
            if normalize:
                for t in range(NT):
                    outs.append(S.op("ve", lambda e, t=t: e.tensor_scalar(
                        src_tok[:, t * D:(t + 1) * D], src_tok[:, t * D:(t + 1) * D],
                        rstd[:, t:t + 1], nmr[:, t:t + 1], ALU.mult, ALU.add),
                        deps=[t7, t5]))
            if out_tok is not None:
                for t in range(NT):
                    buf = scr if t % 2 == 0 else scr2
                    a = S.op("ve", lambda e, t=t, buf=buf: e.tensor_scalar(
                        buf[:, :], src_tok[:, t * D:(t + 1) * D],
                        rstd[:, t:t + 1], nmr[:, t:t + 1], ALU.mult, ALU.add),
                        deps=[t7, t5] + ([outs[t - 2]] if t >= 2 else []))
                    b = S.op("ve", lambda e, buf=buf: e.tensor_mul(
                        buf[:, :], buf[:, :], gamma[:, :]), deps=[a])
                    c = S.op("ve", lambda e, t=t, buf=buf: e.tensor_add(
                        out_tok[:, t * D:(t + 1) * D], buf[:, :], beta[:, :]),
                        deps=[b] + list(out_guards))
                    outs.append(c)
            return outs, t7, t5

        def mm_pipe(units, mk_mms, mk_evac, extra_mm_deps=lambda u: [],
                    extra_ev_deps=lambda u: [], banks=(0, 1, 2), eng_ev="ve"):
            ev_toks = []
            mm_toks = []
            nb = len(banks)
            for n, u in enumerate(units):
                ps = pb[banks[n % nb]]
                deps = list(extra_mm_deps(u))
                if n >= nb:
                    deps.append(ev_toks[n - nb])
                mm_toks.append(S.op("te", lambda e, u=u, ps=ps: mk_mms(e, u, ps),
                                    deps=deps))
                ev_toks.append(S.op(eng_ev, lambda e, u=u, ps=ps: mk_evac(e, u, ps),
                                    deps=[mm_toks[-1]] + list(extra_ev_deps(u))))
            return mm_toks, ev_toks

        def wload(dst, srct, rows, cols, deps=()):
            kc = rows // 128
            return S.dma("wt", lambda e: e.dma_start(
                out=AP(dst, 0, [[kc * cols, 128], [cols, kc], [1, cols]]),
                in_=AP(srct.tensor, 0, [[cols, 128], [128 * cols, kc], [1, cols]])),
                deps=deps)

        xn_tok = al.alloc("xn_tok", [128, NT * D], F32)
        xn_fm = al.alloc("xn_fm", [128, DC * L], BF16)
        xload = []
        for h in range(2):
            xload.append(S.dma("ld", lambda e, h=h: e.dma_start(
                out=AP(xn_tok, h * 8 * D, [[NT * D, 128], [D, 8], [1, D]]),
                in_=AP(x.tensor, h * 8 * 128 * D, [[D, 128], [128 * D, 8], [1, D]]))))
        tdeps = [[xload[t // 8]] for t in range(NT)]
        nrm0, _, ln0_t5 = ln_pass("ln0", xn_tok, rstd0, nmr0, tdeps, normalize=True)

        xnfm_ev = {}
        xnfm_tp = {}
        for t in range(NT):
            for dc in range(DC):
                n = t * DC + dc
                deps = [nrm0[t], t_if]
                if n >= 2:
                    deps.append(xnfm_ev[n - 2])
                xnfm_tp[n] = S.op("te", lambda e, t=t, dc=dc, n=n: e.transpose(
                    pb[4 + n % 2][:, 0:128],
                    xn_tok[:, t * D + dc * 128: t * D + (dc + 1) * 128],
                    ident_f[:, :]), deps=deps)
                xnfm_ev[n] = S.op("ve", lambda e, t=t, dc=dc, n=n: e.tensor_copy(
                    xn_fm[:, dc * L + t * 128: dc * L + (t + 1) * 128],
                    pb[4 + n % 2][:, 0:128]), deps=[xnfm_tp[n]])
        t_xnfm = xnfm_ev[NT * DC - 1]
        dump("xn_tok", xn_tok)
        dump("xn_fm", xn_fm)
        if upto == "ph0":
            return _finish(S, blk, dbg_marks)

        latent = al.alloc("latent", [128, CLC * L], BF16)
        Q_fm = al.alloc("Q_fm", [128, DC * L], BF16)
        K_fm = al.alloc("K_fm", [128, DC * L], BF16)
        wq_sb = al.alloc("wq_sb", [128, DC * D], BF16)
        wkv_sb = al.alloc("wkv_sb", [128, DC * CL], BF16)
        wkup_sb = al.alloc("wkup_sb", [128, CLC * D], BF16)
        wvup_sb = al.alloc("wvup_sb", [128, CLC * D], BF16)

        t_wkv = wload(wkv_sb, wkv, D, CL)
        t_wq = wload(wq_sb, wq, D, D)
        t_wkup = wload(wkup_sb, wkup, CL, D)
        t_wvup = wload(wvup_sb, wvup, CL, D)

        def lat_mms(e, u, ps):
            mc, tt = u
            for dc in range(DC):
                ins = e.matmul(ps[:, 0:512],
                               wkv_sb[:, dc * CL + mc * 128: dc * CL + (mc + 1) * 128],
                               xn_fm[:, dc * L + tt * 512: dc * L + (tt + 1) * 512],
                               start=(dc == 0), stop=(dc == DC - 1))
            return ins

        def lat_ev(e, u, ps):
            mc, tt = u
            return e.tensor_copy(latent[:, mc * L + tt * 512: mc * L + (tt + 1) * 512],
                                 ps[:, 0:512])

        lat_mm, lat_evt = mm_pipe([(mc, tt) for mc in range(CLC) for tt in range(4)],
                                  lat_mms, lat_ev,
                                  extra_mm_deps=lambda u: [t_wkv,
                                                           xnfm_ev[(4 * u[1] + 4) * DC - 1]])

        def q_mms(e, u, ps):
            mc, tt = u
            for dc in range(DC):
                ins = e.matmul(ps[:, 0:512],
                               wq_sb[:, dc * D + mc * 128: dc * D + (mc + 1) * 128],
                               xn_fm[:, dc * L + tt * 512: dc * L + (tt + 1) * 512],
                               start=(dc == 0), stop=(dc == DC - 1))
            return ins

        def q_ev(e, u, ps):
            mc, tt = u
            return e.tensor_scalar_mul(Q_fm[:, mc * L + tt * 512: mc * L + (tt + 1) * 512],
                                       ps[:, 0:512], 0.125)

        q_mm, q_evt = mm_pipe([(mc, tt) for mc in range(DC) for tt in range(4)],
                              q_mms, q_ev,
                              extra_mm_deps=lambda u: [t_wq, lat_evt[-1],
                                                       xnfm_ev[(4 * u[1] + 4) * DC - 1]])

        def k_mms(e, u, ps):
            mc, tt = u
            for cc in range(CLC):
                ins = e.matmul(ps[:, 0:512],
                               wkup_sb[:, cc * D + mc * 128: cc * D + (mc + 1) * 128],
                               latent[:, cc * L + tt * 512: cc * L + (tt + 1) * 512],
                               start=(cc == 0), stop=(cc == CLC - 1))
            return ins

        def k_ev(e, u, ps):
            mc, tt = u
            return e.tensor_copy(K_fm[:, mc * L + tt * 512: mc * L + (tt + 1) * 512],
                                 ps[:, 0:512])

        k_mm, k_evt = mm_pipe([(mc, tt) for mc in range(DC) for tt in range(4)],
                              k_mms, k_ev,
                              extra_mm_deps=lambda u: [t_wkup, lat_evt[-1], q_evt[-1]])
        dump("latent", latent)
        dump("Q_fm", Q_fm)
        dump("K_fm", K_fm)
        if upto == "ph1":
            return _finish(S, blk, dbg_marks)

        al.drop("xn_fm", after=[q_mm[-1]])
        al.drop("wq_sb", after=[q_mm[-1]])
        al.drop("wkv_sb", after=[lat_mm[-1]])
        al.drop("wkup_sb", after=[k_mm[-1]])
        hn_tok = al.alloc("hn_tok", [128, NT * D], BF16)
        g1 = al.alloc("g1rep", [128, D], F32)
        b1 = al.alloc("b1rep", [128, D], F32)
        t_g1 = S.dma("ld2", lambda e: e.dma_start(
            out=g1[:, :], in_=AP(ln1w.tensor, 0, [[0, 128], [1, D]])),
            deps=al.guards["g1rep"])
        t_b1 = S.dma("ld2", lambda e: e.dma_start(
            out=b1[:, :], in_=AP(ln1b.tensor, 0, [[0, 128], [1, D]])),
            deps=al.guards["b1rep"])
        a1 = S.op("ve", lambda e: e.tensor_scalar(
            rstd1[:, :], stats[:, 16:32], 1.0 + EPS, -EPS, ALU.mult, ALU.add),
            deps=[nrm0[0]])
        a1b = S.op("ve", lambda e: e.reciprocal(rstd1[:, :], rstd1[:, :]), deps=[a1])
        a2 = S.op("ve", lambda e: e.tensor_mul(
            rstd1[:, :], stats[:, 16:32], rstd1[:, :]), deps=[a1b])
        a3 = S.op("se", lambda e: e.activation(rstd1[:, :], rstd1[:, :], AF.Sqrt),
                  deps=[a2])
        hn_outs = []
        gd_hntok = al.guards["hn_tok"]
        for t in range(NT):
            buf = scr if t % 2 == 0 else scr2
            ha = S.op("ve", lambda e, t=t, buf=buf: e.scalar_tensor_tensor(
                buf[:, :], xn_tok[:, t * D:(t + 1) * D],
                rstd1[:, t:t + 1], g1[:, :], ALU.mult, ALU.mult),
                deps=[a3, nrm0[t], t_g1, t_b1] +
                     ([hn_outs[t - 2]] if t >= 2 else []))
            hc = S.op("ve", lambda e, t=t, buf=buf: e.tensor_add(
                hn_tok[:, t * D:(t + 1) * D], buf[:, :], b1[:, :]),
                deps=[ha] + gd_hntok)
            hn_outs.append(hc)
        al.drop("g1rep", "b1rep", after=[hn_outs[-1]])
        al.drop("xn_tok", after=[hn_outs[-1]])
        hn_fm = al.alloc("hn_fm", [128, DC * L], BF16)
        gd_hnfm = al.guards["hn_fm"]
        hnfm_ev = {}
        hnfm_tp = {}
        for t in range(NT):
            for dc in range(DC):
                n = t * DC + dc
                deps = [hn_outs[t], t_ib]
                if n >= 2:
                    deps.append(hnfm_ev[n - 2])
                hnfm_tp[n] = S.op("te", lambda e, t=t, dc=dc, n=n: e.transpose(
                    pbT[n % 2][:, 0:128],
                    hn_tok[:, t * D + dc * 128: t * D + (dc + 1) * 128],
                    ident_b[:, :]), deps=deps)
                hnfm_ev[n] = S.op("ve", lambda e, t=t, dc=dc, n=n: e.tensor_copy(
                    hn_fm[:, dc * L + t * 128: dc * L + (t + 1) * 128],
                    pbT[n % 2][:, 0:128]),
                    deps=[hnfm_tp[n]] + gd_hnfm)
        t_hnfm = hnfm_ev[NT * DC - 1]
        al.drop("hn_tok", after=[hnfm_tp[NT * DC - 1]])

        wip_sb = al.alloc("wip_sb", [128, DC * 3 * D], BF16)
        ipb_rep = al.alloc("ipb_rep", [128, 3 * D], F32)
        ipbq_rep = al.alloc("ipbq_rep", [128, D], F32)
        qt = [al.alloc(f"qt{i}", [128, D], BF16) for i in range(2)]
        kvt = [al.alloc(f"kvt{i}", [128, 2 * D], BF16) for i in range(2)]
        t_wip = S.dma("wt", lambda e: e.dma_start(
            out=AP(wip_sb, 0, [[DC * 3 * D, 128], [3 * D, DC], [1, 3 * D]]),
            in_=AP(wipT.tensor, 0, [[3 * D, 128], [128 * 3 * D, DC], [1, 3 * D]])),
            deps=al.guards["wip_sb"])
        t_ipb = S.dma("ld2", lambda e: e.dma_start(
            out=ipb_rep[:, :], in_=AP(ipb.tensor, 0, [[0, 128], [1, 3 * D]])),
            deps=al.guards["ipb_rep"])
        t_ipbq = S.op("ve", lambda e: e.tensor_scalar_mul(
            ipbq_rep[:, :], ipb_rep[:, 0:D], 0.125),
            deps=[t_ipb] + al.guards["ipbq_rep"])
        gd_qtok = [al.guards["qt0"], al.guards["qt1"]]
        gd_kvt = [al.guards["kvt0"], al.guards["kvt1"]]
        kv_store = {}
        q_store = {}

        def ip_mms(e, u, ps):
            tb, sl = u
            for dc in range(DC):
                ins = e.matmul(ps[:, 0:512],
                               hn_fm[:, dc * L + tb * 128: dc * L + (tb + 1) * 128],
                               wip_sb[:, dc * 3 * D + sl * 512: dc * 3 * D + (sl + 1) * 512],
                               start=(dc == 0), stop=(dc == DC - 1))
            return ins

        def ip_ev(e, u, ps):
            tb, sl = u
            if sl < 2:
                dst = qt[tb % 2][:, sl * 512:(sl + 1) * 512]
                return e.scalar_tensor_tensor(dst, ps[:, 0:512], 0.125,
                                              ipbq_rep[:, sl * 512:(sl + 1) * 512],
                                              ALU.mult, ALU.add)
            dst = kvt[tb % 2][:, (sl - 2) * 512:(sl - 1) * 512]
            return e.scalar_tensor_tensor(dst, ps[:, 0:512], 1.0,
                                          ipb_rep[:, sl * 512:(sl + 1) * 512],
                                          ALU.mult, ALU.add)

        def ip_evdeps(u):
            tb, sl = u
            d = [t_ipb, t_ipbq]
            d += gd_qtok[tb % 2] if sl < 2 else gd_kvt[tb % 2]
            if sl == 0 and tb >= 2:
                d.append(q_store[tb - 2])
            if sl == 2 and tb >= 2:
                d.append(kv_store[tb - 2])
            return d

        ip_mm = []
        ip_evt = []
        ip_order = [(tb, sl) for tb in range(NT) for sl in (2, 3, 4, 5)] + \
                   [(tb, sl) for tb in range(NT) for sl in (0, 1)]
        for n, u in enumerate(ip_order):
            tb, sl = u
            ps = pb[n % 3]
            deps = [t_wip, hnfm_ev[(tb + 1) * DC - 1], k_evt[-1]]
            if n >= 3:
                deps.append(ip_evt[n - 3])
            ip_mm.append(S.op("te", lambda e, u=u, ps=ps: ip_mms(e, u, ps), deps=deps))
            ip_evt.append(S.op("ve", lambda e, u=u, ps=ps: ip_ev(e, u, ps),
                               deps=[ip_mm[-1]] + list(ip_evdeps(u))))
            if sl == 1:
                q_store[tb] = S.dma("st", lambda e, tb=tb: e.dma_start(
                    out=q_sp[tb * 128:(tb + 1) * 128, :], in_=qt[tb % 2][:, :]),
                    deps=[ip_evt[-1]])
            if sl == 5:
                kv_store[tb] = S.dma("st", lambda e, tb=tb: e.dma_start(
                    out=ag_in[tb * 128:(tb + 1) * 128, :], in_=kvt[tb % 2][:, :]),
                    deps=[ip_evt[-1]])
        al.drop("hn_fm", after=[ip_mm[-1]])
        t_ag = S.collective(lambda e: e.collective_compute(
            "AllGather", ALU.bypass, replica_groups=[list(range(8))],
            ins=[ag_in.opt()], outs=[ag_out.opt()]),
            deps=[kv_store[tb] for tb in range(NT)])
        al.drop("wip_sb", after=[ip_mm[-1]])
        al.drop("ipb_rep", "ipbq_rep", after=[ip_evt[-1]])
        al.drop("kvt0", "kvt1", after=[kv_store[14], kv_store[15]])
        al.drop("qt0", "qt1", after=[q_store[14], q_store[15]])
        dump("hn_fm", hn_fm)
        if upto == "ph1b":
            return _finish(S, blk, dbg_marks)

        V_sb = al.alloc("V_sb", [128, NT * D], BF16)
        gd_v = al.guards["V_sb"]
        y_fm = al.alloc("y_fm", [128, DC * L], BF16)
        gd_yfm = al.guards["y_fm"]
        def v_mms(e, u, ps):
            tb, half = u
            for cc in range(CLC):
                ins = e.matmul(ps[:, 0:512],
                               latent[:, cc * L + tb * 128: cc * L + (tb + 1) * 128],
                               wvup_sb[:, cc * D + half * 512: cc * D + (half + 1) * 512],
                               start=(cc == 0), stop=(cc == CLC - 1))
            return ins

        def v_ev(e, u, ps):
            tb, half = u
            return e.tensor_copy(V_sb[:, tb * D + half * 512: tb * D + (half + 1) * 512],
                                 ps[:, 0:512])

        v_mm, v_evt = mm_pipe([(tb, half) for tb in range(NT) for half in range(2)],
                              v_mms, v_ev,
                              extra_mm_deps=lambda u: [t_wvup, lat_evt[-1], k_evt[-1],
                                                       ip_evt[-1]],
                              extra_ev_deps=lambda u: gd_v)
        t_vall = v_evt[-1]
        al.drop("latent", after=[v_mm[-1]])
        al.drop("wvup_sb", after=[v_mm[-1]])
        dump("V_sb", V_sb)

        kvk = al.alloc("kvk", [128, 4 * D], BF16)
        kvr = al.alloc("kvr", [128, 8 * D], BF16)
        qbp = [al.alloc(f"qbp{i}", [128, D], BF16) for i in range(2)]
        tmpg = [al.alloc(f"tmpg{i}", [128, 2 * D], BF16) for i in range(2)]
        scp = [al.alloc(f"scp{i}", [128, 128], F32) for i in range(2)]
        smp = al.alloc("smp", [128, 32], F32)
        tmpv = [al.alloc(f"tmpv{i}", [128, D], BF16) for i in range(2)]
        accp = [al.alloc(f"accp{i}", [128, D], F32) for i in range(2)]
        gd_pump = {n: al.guards[n] for n in
                   ("kvr", "kvk", "qbp0", "qbp1", "tmpg0", "tmpg1", "scp0",
                    "scp1", "smp", "tmpv0", "tmpv1", "accp0", "accp1")}
        acc_store = {}

        def mk_kload(lt, half, deps):
            return S.dma("kv", lambda e, lt=lt, half=half: e.dma_start(
                out=AP(kvk, 0, [[4 * D, 128], [D, 4], [1, D]]),
                in_=AP(ag_out.tensor, (4 * half * L + lt * 128) * 2 * D,
                       [[2 * D, 128], [L * 2 * D, 4], [1, D]])), deps=deps)

        def pump_gen():
            prev_mul = {}
            prev_red = {}
            prev_pvm = {}
            prev_pva = {}
            prev_norm = {}
            prev_acst = {}
            for lt in range(NT):
                sc4 = scp[lt % 2]
                t_q = S.dma("ld", lambda e, lt=lt: e.dma_start(
                    out=qbp[lt % 2][:, :], in_=q_sp[lt * 128:(lt + 1) * 128, :]),
                    deps=[q_store[lt]] + gd_pump[f"qbp{lt % 2}"] +
                         ([prev_mul[lt - 2]] if lt >= 2 else []))
                kvds = {}
                kvds[0] = mk_kload(lt, 0, [t_ag] + gd_pump["kvk"] +
                                   ([prev_mul[lt - 1]] if lt >= 1 else []))
                t_vld = S.dma("kv", lambda e, lt=lt: e.dma_start(
                    out=AP(kvr, 0, [[8 * D, 128], [D, 8], [1, D]]),
                    in_=AP(ag_out.tensor, lt * 128 * 2 * D + D,
                           [[2 * D, 128], [L * 2 * D, 8], [1, D]]),
                ), deps=[t_ag] + gd_pump["kvr"] +
                        ([prev_pva[lt - 1]] if lt >= 1 else []))
                for p4 in range(4):
                    if p4 == 2:
                        kvds[1] = mk_kload(lt, 1, [prev_mul[lt]])
                    buf = tmpg[p4 % 2]
                    for j in range(2):
                        t = 2 * p4 + j
                        half, t4 = t // 4, t % 4
                        mdeps = [kvds[half], t_q]
                        if lt == 0 and p4 < 2 and j == 0:
                            mdeps += gd_pump[f"tmpg{p4 % 2}"]
                        if p4 >= 2 and j == 0:
                            mdeps.append(prev_red[(lt, p4 - 2)])
                        if lt >= 1 and p4 < 2 and j == 0:
                            mdeps.append(prev_red[(lt - 1, p4 + 2)])
                        prev_mul[lt] = S.op("ge", lambda e, lt=lt, t4=t4,
                                            j=j, buf=buf: e.tensor_mul(
                            buf[:, j * D:(j + 1) * D], qbp[lt % 2][:, :],
                            kvk[:, t4 * D:(t4 + 1) * D]), deps=mdeps)
                    rdeps = [prev_mul[lt]]
                    if lt < 2 and p4 == 0:
                        rdeps += gd_pump[f"scp{lt % 2}"]
                    if lt >= 2 and p4 == 0:
                        rdeps.append(prev_norm[lt - 2])
                    prev_red[(lt, p4)] = S.op("ve", lambda e, p4=p4, sc4=sc4, buf=buf:
                                              e.tensor_reduce(
                        AP(sc4, 2 * p4, [[128, 128], [1, 2], [8, 16]]),
                        AP(buf, 0, [[2 * D, 128], [D, 2], [64, 16], [1, 64]]),
                        AX.X, ALU.add), deps=rdeps)
                    yield
                c1 = S.op("ve", lambda e, sc4=sc4: e.tensor_add(
                    sc4[:, :], sc4[:, :], tmaskrep[:, :]),
                    deps=[prev_red[(lt, 3)], t_tm])
                yield
                c4 = S.op("se", lambda e, sc4=sc4: e.activation(
                    sc4[:, :], sc4[:, :], AF.Exp), deps=[c1])
                c5 = S.op("ve", lambda e, sc4=sc4: e.tensor_reduce(
                    smp[:, 0:16], AP(sc4, 0, [[128, 128], [8, 16], [1, 8]]),
                    AX.X, ALU.add), deps=[c4] + (gd_pump["smp"] if lt == 0 else [])
                    + ([prev_norm[lt - 1]] if lt >= 1 else []))
                c6 = S.op("ve", lambda e: e.reciprocal(smp[:, 16:32], smp[:, 0:16]),
                          deps=[c5])
                yield
                pv_add = None
                for t in range(8):
                    half, t4 = t // 4, t % 4
                    vdeps = [c4]
                    if lt == 0 and t < 2:
                        vdeps += gd_pump[f"tmpv{t % 2}"]
                    if t >= 2:
                        vdeps.append(prev_pva[lt])
                    pv_m = S.op("ve", lambda e, t=t, sc4=sc4: e.tensor_mul(
                        tmpv[t % 2][:, :],
                        kvr[:, t * D:(t + 1) * D],
                        AP(sc4, t, [[128, 128], [8, 16], [0, 64]])),
                        deps=vdeps + [t_vld])
                    adeps = [pv_m]
                    if t == 0:
                        adeps += (gd_pump[f"accp{lt % 2}"] if lt < 2 else
                                  [prev_acst[lt - 2]])
                    else:
                        adeps.append(prev_pva[lt])
                    pv_add = S.op("ge", lambda e, lt=lt, t=t: (
                        e.tensor_copy(accp[lt % 2][:, :], tmpv[t % 2][:, :]) if t == 0
                        else e.tensor_add(accp[lt % 2][:, :], accp[lt % 2][:, :],
                                          tmpv[t % 2][:, :])), deps=adeps)
                    prev_pva[lt] = pv_add
                    if t % 2 == 1:
                        yield
                prev_norm[lt] = S.op("ve", lambda e, lt=lt: e.tensor_tensor(
                    accp[lt % 2][:, :], accp[lt % 2][:, :],
                    AP(smp, 16, [[32, 128], [1, 16], [0, 64]]), ALU.mult),
                    deps=[pv_add, c6])
                yield
                acc_store[lt] = S.dma("st", lambda e, lt=lt: e.dma_start(
                    out=acc_sp[lt * 128:(lt + 1) * 128, :], in_=accp[lt % 2][:, :]),
                    deps=[prev_norm[lt]])
                prev_acst[lt] = acc_store[lt]
            while True:
                yield

        pump = pump_gen()
        pump_left = [NT * 11 + 5]

        def pump_step(n):
            for _ in range(n):
                if pump_left[0] > 0:
                    next(pump)
                    pump_left[0] -= 1




        P_sb = [al.alloc(f"P{i}", [128, 512], BF16) for i in range(4)]
        dnrb = al.alloc("dnrb", [128, 1024], BF16)

        gd_P = [al.guards[f"P{i}"] for i in range(4)]

        gd_dnrb = al.guards["dnrb"]
        last_exp = {}
        prev_yse = [None, None]
        prev_m12 = [None, None]
        prev_r = [None, None]
        prev_tt = [None, None]
        unit_no = [0]
        PUMP_START = 8
        for hp in range(8):
            for qb in range(4):
                ks = []
                for kb in range(4 * qb + 4):
                    j = kb - 4 * qb
                    ks.append((kb, 128 * j if j >= 0 else 0))
                nk = len(ks)
                exp_t = {}
                av_t = {}
                sc_t = {}
                for i, (kb, off) in enumerate(ks):
                    bnd = off > 0 or kb == 4 * qb
                    for par in range(2):
                        sbank = pb[2 + par]
                        deps = [q_evt[-1], k_evt[-1], t_trib, t_ib, t_vall]
                        if (0, par) in last_exp:
                            deps.append(last_exp[(0, par)])

                        def scfn(e, hp=hp, qb=qb, kb=kb, off=off, par=par,
                                 sbank=sbank, bnd=bnd):
                            ins = e.matmul(
                                sbank[:, off:512],
                                K_fm[par * 64:par * 64 + 64,
                                     hp * L + kb * 128: hp * L + (kb + 1) * 128],
                                Q_fm[par * 64:par * 64 + 64,
                                     hp * L + qb * 512 + off: hp * L + (qb + 1) * 512],
                                start=True, stop=not bnd)
                            if bnd:
                                ins = e.matmul(sbank[:, off:off + 128],
                                               ident_b[:, :], tri_b[:, :],
                                               start=False, stop=True)
                            return ins
                        sc_t[(i, par)] = S.op("te", scfn, deps=deps)
                    for par in range(2):
                        sbank = pb[2 + par]
                        deps = [sc_t[(i, par)]]
                        if i >= 2:
                            deps.append(av_t[(i - 2, par, "den")])
                        deps += gd_P[(i % 2) * 2 + par]
                        exp_t[(i, par)] = S.op("se", lambda e, off=off, sbank=sbank,
                                               par=par, i=i: e.activation(
                            P_sb[(i % 2) * 2 + par][:, off:512],
                            sbank[:, off:512], AF.Exp, bias=cshb[:, 0:1]),
                            deps=deps + [t_csh])
                        last_exp[(0, par)] = exp_t[(i, par)]
                    for par in range(2):
                        h = 2 * hp + par
                        ybank = pb[0] if par == 0 else pb[4]
                        dbank = pb[1] if par == 0 else pb[5]
                        deps = [exp_t[(i, par)], t_vall]
                        if i == 0 and prev_yse[par] is not None:
                            deps.append(prev_yse[par])
                            deps.append(prev_m12[par])
                            deps.append(prev_tt[par])
                        av_t[(i, par, "y")] = S.op("te", lambda e, kb=kb, off=off,
                                                   par=par, h=h, i=i, nk=nk,
                                                   ybank=ybank: e.matmul(
                            ybank[par * 64:(par + 1) * 64, off:512],
                            V_sb[:, kb * D + h * HD: kb * D + (h + 1) * HD],
                            P_sb[(i % 2) * 2 + par][:, off:512],
                            start=(i == 0), stop=(i == nk - 1),
                            tile_position=(0, par * 64)), deps=deps)
                        av_t[(i, par, "den")] = S.op("te", lambda e, off=off, par=par,
                                                     i=i, nk=nk, dbank=dbank: e.matmul(
                            dbank[0:1, off:512],
                            ones_b[:, 0:1],
                            P_sb[(i % 2) * 2 + par][:, off:512],
                            start=(i == 0), stop=(i == nk - 1)),
                            deps=[exp_t[(i, par)], t_ones_b, t_vall] +
                                 ([prev_r[par]] if (i == 0 and prev_r[par] is not None)
                                  else []))
                rr0 = S.op("ve", lambda e: e.reciprocal(
                    scr[0:1, 0:512], pb[1][0:1, 0:512]),
                    deps=[av_t[(nk - 1, 0, "den")]] +
                         ([prev_r[1]] if prev_r[1] is not None else [ip_evt[-1]]))
                prev_r[0] = rr0
                rr1 = S.op("ve", lambda e: e.reciprocal(
                    scr[0:1, 512:1024], pb[5][0:1, 0:512]),
                    deps=[av_t[(nk - 1, 1, "den")], rr0])
                rcast = S.op("ve", lambda e: e.tensor_copy(dnrb[0:1, :], scr[0:1, :]),
                             deps=[rr1] + gd_dnrb)
                prev_r[1] = rcast
                yse = [None, None]
                yse[0] = S.op("se", lambda e, hp=hp, qb=qb: e.activation(
                    y_fm[0:64, hp * L + qb * 512: hp * L + (qb + 1) * 512],
                    pb[0][0:64, 0:512], AF.Copy),
                    deps=[av_t[(nk - 1, 0, "y")]] + gd_yfm)
                yse[1] = S.op("se", lambda e, hp=hp, qb=qb: e.activation(
                    y_fm[64:128, hp * L + qb * 512: hp * L + (qb + 1) * 512],
                    pb[4][64:128, 0:512], AF.Copy),
                    deps=[av_t[(nk - 1, 1, "y")]] + gd_yfm)
                m1 = S.op("te", lambda e: e.matmul(
                    pb[0][0:64, 0:512], ones_b[0:1, 0:64], dnrb[0:1, 0:512],
                    start=True, stop=True),
                    deps=[rcast, yse[0]])
                m2 = S.op("te", lambda e: e.matmul(
                    pb[0][64:128, 0:512], ones_b[0:1, 0:64], dnrb[0:1, 512:1024],
                    start=True, stop=True, tile_position=(0, 64)),
                    deps=[m1] + ([prev_tt[1]] if prev_tt[1] is not None else []))
                prev_m12 = [m1, m2]
                prev_tt[0] = S.op("ve", lambda e, hp=hp, qb=qb: e.tensor_tensor(
                    y_fm[0:64, hp * L + qb * 512: hp * L + (qb + 1) * 512],
                    y_fm[0:64, hp * L + qb * 512: hp * L + (qb + 1) * 512],
                    pb[0][0:64, 0:512], ALU.mult), deps=[m1, yse[0]])
                prev_tt[1] = S.op("ve", lambda e, hp=hp, qb=qb: e.tensor_tensor(
                    y_fm[64:128, hp * L + qb * 512: hp * L + (qb + 1) * 512],
                    y_fm[64:128, hp * L + qb * 512: hp * L + (qb + 1) * 512],
                    pb[0][64:128, 0:512], ALU.mult), deps=[m2, yse[1], prev_tt[0]])
                prev_yse = yse
                unit_no[0] += 1
                if unit_no[0] > PUMP_START:
                    pump_step(7)
        prev_unit_yev = prev_tt[1]
        t_yfm = prev_unit_yev
        t_att_te = m2
        t_att_se = max(last_exp.values(), key=lambda t: t.val)
        al.drop("P0", "P1", "P2", "P3", after=[t_att_te])
        al.drop("dnrb", after=[t_att_te, t_yfm])

        al.drop("Q_fm", "K_fm", "V_sb", after=[t_att_te])
        dump("y_fm", y_fm)
        if upto == "ph2":
            return _finish(S, blk, dbg_marks)

        wout_sb = al.alloc("wout_sb", [128, DC * D], BF16)
        stile = [al.alloc(f"stile{i}", [128, D], F32) for i in range(2)]
        t_wout = wload(wout_sb, wout, D, D, deps=al.guards["wout_sb"])
        gd_st = [al.guards["stile0"], al.guards["stile1"]]
        s_store = {}

        def s_mms(e, u, ps):
            tb, half = u
            for dc in range(DC):
                ins = e.matmul(ps[:, 0:512],
                               y_fm[:, dc * L + tb * 128: dc * L + (tb + 1) * 128],
                               wout_sb[:, dc * D + half * 512: dc * D + (half + 1) * 512],
                               start=(dc == 0), stop=(dc == DC - 1))
            return ins

        def s_ev(e, u, ps):
            tb, half = u
            return e.activation(stile[tb % 2][:, half * 512:(half + 1) * 512],
                                ps[:, 0:512], AF.Copy)

        def s_evdeps(u):
            tb, half = u
            d = list(gd_st[tb % 2])
            if half == 0 and tb >= 2:
                d.append(s_store[tb - 2])
            return d

        s_mm = []
        s_evt = []
        for n, u in enumerate([(tb, half) for tb in range(NT) for half in range(2)]):
            tb, half = u
            ps = pb[n % 2]
            deps = [t_wout, t_yfm]
            if n >= 2:
                deps.append(s_evt[n - 2])
            s_mm.append(S.op("te", lambda e, u=u, ps=ps: s_mms(e, u, ps), deps=deps))
            pump_step(4)
            s_evt.append(S.op("se", lambda e, u=u, ps=ps: s_ev(e, u, ps),
                              deps=[s_mm[-1]] + s_evdeps(u)))
            if half == 1:
                s_store[tb] = S.dma("st", lambda e, tb=tb: e.dma_start(
                    out=s_sp[tb * 128:(tb + 1) * 128, :], in_=stile[tb % 2][:, :]),
                    deps=[s_evt[-1]])
        pump_step(NT * 11 + 5)
        al.drop("y_fm", "wout_sb", after=[s_mm[-1]])
        al.drop("stile0", "stile1", after=[s_store[14], s_store[15]])
        if upto == "ph3":
            return _finish(S, blk, dbg_marks,
                           extra=[s_store[tb] for tb in range(NT)])

        al.drop("kvr", "kvk", "qbp0", "qbp1", "tmpg0", "tmpg1", "scp0", "scp1",
                "smp", "tmpv0", "tmpv1", "accp0", "accp1",
                after=[acc_store[NT - 2], acc_store[NT - 1],
                       list(acc_store.values())[-1]])
        accl = [al.alloc(f"accl{i}", [128, D], F32) for i in range(2)]
        xb4 = [al.alloc(f"xb4{i}", [128, D], F32) for i in range(2)]
        wop_sb = al.alloc("wop_sb", [128, DC * D], BF16)
        opb_rep = al.alloc("opb_rep", [128, D], F32)
        accfm = al.alloc("accfm", [128, D], BF16)
        tmpf4 = al.alloc("tmpf4", [128, D], F32)
        x1t = [al.alloc(f"x1t{i}", [128, D], F32) for i in range(2)]
        t_wop = wload(wop_sb, wopT, D, D, deps=al.guards["wop_sb"])
        t_opb = S.dma("ld2", lambda e: e.dma_start(
            out=opb_rep[:, :], in_=AP(opb.tensor, 0, [[0, 128], [1, D]])),
            deps=al.guards["opb_rep"])
        gd4 = {n: al.guards[n] for n in ("accl0", "accl1", "xb40", "xb41",
                                         "accfm", "tmpf4", "x1t0", "x1t1")}
        x1_store = {}
        last_x1 = {}
        prev_mm_halves = [t_att_se, t_yfm]
        prev_ev_last = None
        for lt in range(NT):
            t_al = S.dma("ld", lambda e, lt=lt: e.dma_start(
                out=accl[lt % 2][:, :], in_=acc_sp[lt * 128:(lt + 1) * 128, :]),
                deps=[acc_store[lt]] + gd4[f"accl{lt % 2}"] +
                     ([last_x1[lt - 2]] if lt >= 2 else []))
            t_x = S.dma("ld2", lambda e, lt=lt: e.dma_start(
                out=xb4[lt % 2][:, :], in_=x[lt * 128:(lt + 1) * 128, :]),
                deps=gd4[f"xb4{lt % 2}"] +
                     ([last_x1[lt - 2]] if lt >= 2 else []))
            tps = []
            evs = []
            for dc in range(DC):
                tpd = [t_al, t_if] + prev_mm_halves
                if dc >= 2:
                    tpd.append(evs[dc - 2])
                tps.append(S.op("te", lambda e, dc=dc, lt=lt: e.transpose(
                    pb[2 + (dc % 2)][:, 0:128], accl[lt % 2][:, dc * 128:(dc + 1) * 128],
                    ident_f[:, :]), deps=tpd))
                evs.append(S.op("se", lambda e, dc=dc: e.activation(
                    accfm[:, dc * 128:(dc + 1) * 128], pb[2 + (dc % 2)][:, 0:128],
                    AF.Copy), deps=[tps[dc]] +
                    (gd4["accfm"] if (lt == 0 and dc == 0) else
                     ([prev_ev_last] if (dc == 0 and prev_ev_last is not None) else []))))
            prev_ev_last = evs[-1]
            mm_halves = []
            for half in range(2):
                def opmm(e, half=half, lt=lt):
                    for dc2 in range(DC):
                        ins = e.matmul(
                            pb[4 + half][:, 0:512],
                            accfm[:, dc2 * 128:(dc2 + 1) * 128],
                            wop_sb[:, dc2 * D + half * 512: dc2 * D + (half + 1) * 512],
                            start=(dc2 == 0), stop=(dc2 == DC - 1))
                    return ins
                mm_halves.append(S.op("te", opmm,
                                      deps=[evs[-1], t_wop] +
                                      ([last_x1[lt - 1]] if lt >= 1
                                       else [t_att_se, t_yfm])))
            prev_mm_halves = mm_halves
            xn1 = S.op("ve", lambda e, lt=lt: e.tensor_scalar(
                tmpf4[:, :], xb4[lt % 2][:, :], rstd0[:, lt:lt + 1],
                nmr0[:, lt:lt + 1], ALU.mult, ALU.add),
                deps=[t_x] + (gd4["tmpf4"] if lt == 0 else [last_x1[lt - 1]]))
            hdeps = []
            for half in range(2):
                hdeps.append(S.op("ve", lambda e, lt=lt, half=half: e.tensor_add(
                    x1t[lt % 2][:, half * 512:(half + 1) * 512],
                    pb[4 + half][:, 0:512], tmpf4[:, half * 512:(half + 1) * 512]),
                    deps=[xn1, mm_halves[half]] +
                    (gd4[f"x1t{lt % 2}"] if lt < 2 else [x1_store[lt - 2]])))
            t_x1 = S.op("ve", lambda e, lt=lt: e.tensor_add(
                x1t[lt % 2][:, :], x1t[lt % 2][:, :], opb_rep[:, :]),
                deps=hdeps + [t_opb])
            last_x1[lt] = t_x1
            x1_store[lt] = S.dma("st", lambda e, lt=lt: e.dma_start(
                out=x1_sp[lt * 128:(lt + 1) * 128, :], in_=x1t[lt % 2][:, :]),
                deps=[t_x1])
        ph4_guard = [last_x1[NT - 1], x1_store[NT - 2], x1_store[NT - 1],
                     prev_mm_halves[-1], evs[-1]]
        al.drop("accl0", "accl1", "xb40", "xb41", "wop_sb", "opb_rep",
                "accfm", "tmpf4", "x1t0", "x1t1", after=ph4_guard)

        if upto == "ph4":
            if "agio" in dbg:
                dai = nc.dram_tensor("dbg_agin", [L, 2 * D], BF16, kind="ExternalOutput").ap()
                dao = nc.dram_tensor("dbg_agout", [8 * L, 2 * D], BF16, kind="ExternalOutput").ap()
                S.dma("st", lambda e: e.dma_start(out=dai, in_=ag_in),
                      deps=[kv_store[tb] for tb in range(NT)])
                S.dma("st", lambda e: e.dma_start(out=dao, in_=ag_out), deps=[t_ag])
            if "x1" in dbg:
                dx = nc.dram_tensor("dbg_x1", [L, D], F32, kind="ExternalOutput").ap()
                S.dma("st", lambda e: e.dma_start(out=dx, in_=x1_sp),
                      deps=[x1_store[lt] for lt in range(NT)])
            return _finish(S, blk, dbg_marks,
                           extra=[x1_store[lt] for lt in range(NT)])

        x1_tok = al.alloc("x1_tok", [128, NT * D], F32)
        g2 = al.alloc("g2rep", [128, D], F32)
        b2 = al.alloc("b2rep", [128, D], F32)
        hn2_tok = al.alloc("hn2_tok", [128, NT * D], BF16)
        t_g2 = S.dma("ld2", lambda e: e.dma_start(
            out=g2[:, :], in_=AP(ln2w.tensor, 0, [[0, 128], [1, D]])),
            deps=al.guards["g2rep"])
        t_b2 = S.dma("ld2", lambda e: e.dma_start(
            out=b2[:, :], in_=AP(ln2b.tensor, 0, [[0, 128], [1, D]])),
            deps=al.guards["b2rep"])
        x1ld = []
        for h in range(2):
            x1ld.append(S.dma("ld", lambda e, h=h: e.dma_start(
                out=AP(x1_tok, h * 8 * D, [[NT * D, 128], [D, 8], [1, D]]),
                in_=AP(x1_sp.tensor, h * 8 * 128 * D, [[D, 128], [128 * D, 8], [1, D]])),
                deps=[x1_store[h * 8 + i] for i in range(8)] + al.guards["x1_tok"]))
        hn2_outs, _, ln2_t5 = ln_pass(
            "ln2", x1_tok, rstd2, nmr2,
            [[x1ld[t // 8], t_g2, t_b2] for t in range(NT)],
            normalize=False, gamma=g2, beta=b2, out_tok=hn2_tok,
            out_guards=al.guards["hn2_tok"])
        al.drop("g2rep", "b2rep", after=[hn2_outs[-1]])
        al.drop("x1_tok", after=[hn2_outs[-1], ln2_t5])
        hn2_fm = al.alloc("hn2_fm", [128, DC * L], BF16)
        gd_h2 = al.guards["hn2_fm"]
        h2ev = {}
        h2tp = {}
        for t in range(NT):
            for dc in range(DC):
                n = t * DC + dc
                deps = [hn2_outs[t], t_ib, t_att_se]
                if n >= 2:
                    deps.append(h2ev[n - 2])
                h2tp[n] = S.op("te", lambda e, t=t, dc=dc, n=n: e.transpose(
                    pbT[n % 2][:, 0:128],
                    hn2_tok[:, t * D + dc * 128: t * D + (dc + 1) * 128],
                    ident_b[:, :]), deps=deps)
                h2ev[n] = S.op("ve", lambda e, t=t, dc=dc, n=n: e.tensor_copy(
                    hn2_fm[:, dc * L + t * 128: dc * L + (t + 1) * 128],
                    pbT[n % 2][:, 0:128]),
                    deps=[h2tp[n]] + gd_h2)
        t_h2fm = h2ev[NT * DC - 1]
        al.drop("hn2_tok", after=[h2tp[NT * DC - 1]])
        dump("hn2_fm", hn2_fm)

        wr_sb = al.alloc("wr_sb", [128, DC * E], BF16)
        rlog = al.alloc("rlog", [128, NT * E], F32)
        rtmp = al.alloc("rtmp", [128, NT * E], F32)
        rcol = al.alloc("rcol", [128, 2 * NT], F32)
        t_wr = wload(wr_sb, wrT, D, E, deps=al.guards["wr_sb"])
        gd_rl = al.guards["rlog"] + al.guards["rtmp"] + al.guards["rcol"]
        rcop = {}
        prev_mm = None
        for tb in range(NT):
            bidx = 4 + tb % 2
            def rmm(e, tb=tb, bidx=bidx):
                for dc in range(DC):
                    ins = e.matmul(pb[bidx][:, 0:E],
                                   hn2_fm[:, dc * L + tb * 128: dc * L + (tb + 1) * 128],
                                   wr_sb[:, dc * E:(dc + 1) * E],
                                   start=(dc == 0), stop=(dc == DC - 1))
                return ins
            mm = S.op("te", rmm, deps=[t_wr, t_h2fm] +
                      ([rcop[tb - 2]] if tb >= 2 else []))
            rcop[tb] = S.op("se", lambda e, tb=tb, bidx=bidx: e.activation(
                rlog[:, tb * E:(tb + 1) * E], pb[bidx][:, 0:E], AF.Copy),
                deps=[mm] + (gd_rl if tb == 0 else []))
            prev_mm = mm
        e1 = S.op("se", lambda e: e.activation(rlog[:, :], rlog[:, :], AF.Exp),
                  deps=[rcop[NT - 1]])
        v1 = S.op("ve", lambda e: e.tensor_reduce(
            rcol[:, 0:NT], AP(rlog, 0, [[NT * E, 128], [E, NT], [1, E]]),
            AX.X, ALU.add), deps=[e1])
        v2 = S.op("ve", lambda e: e.reciprocal(rcol[:, 0:NT], rcol[:, 0:NT]), deps=[v1])
        v3 = S.op("ve", lambda e: e.tensor_tensor(
            rlog[:, :], rlog[:, :],
            AP(rcol, 0, [[2 * NT, 128], [1, NT], [0, E]]), ALU.mult), deps=[v2])
        v4 = S.op("ve", lambda e: e.tensor_reduce(
            rcol[:, NT:2 * NT], AP(rlog, 0, [[NT * E, 128], [E, NT], [1, E]]),
            AX.X, ALU.max), deps=[v3])
        v5 = S.op("ve", lambda e: e.tensor_tensor(
            rtmp[:, :], rlog[:, :],
            AP(rcol, NT, [[2 * NT, 128], [1, NT], [0, E]]), ALU.is_ge), deps=[v4])
        v6a = S.op("ve", lambda e: e.tensor_scalar(
            rtmp[:, :], rtmp[:, :], -1.0, 1.0, ALU.mult, ALU.add), deps=[v5])
        v6b = S.op("ve", lambda e: e.tensor_mul(rtmp[:, :], rtmp[:, :], rlog[:, :]),
                   deps=[v6a])
        v7 = S.op("ve", lambda e: e.tensor_reduce(
            rcol[:, NT:2 * NT], AP(rtmp, 0, [[NT * E, 128], [E, NT], [1, E]]),
            AX.X, ALU.max), deps=[v6b])
        v8 = S.op("ve", lambda e: e.tensor_tensor(
            rtmp[:, :], rlog[:, :],
            AP(rcol, NT, [[2 * NT, 128], [1, NT], [0, E]]), ALU.is_ge), deps=[v7])
        t_gates = S.op("ve", lambda e: e.tensor_mul(gates[:, :], rtmp[:, :],
                                                    rlog[:, :]), deps=[v8])
        if upto == "ph5":
            return _finish(S, blk, dbg_marks)

        w1_sb = al.alloc("w1_sb", [128, DC * F], BF16)
        w2_sb = al.alloc("w2_sb", [128, FC * D], BF16)
        hmid = [al.alloc(f"hmid{i}", [128, FC * 512], BF16) for i in range(2)]
        y_acc2 = [al.alloc(f"y_acc{i}", [128, 8 * D], F32) for i in range(2)]
        gd_ya = al.guards["y_acc0"] + al.guards["y_acc1"]
        ya = lambda tb: y_acc2[tb // 8][:, (tb % 8) * D:(tb % 8 + 1) * D]
        gd_hm = [al.guards["hmid0"], al.guards["hmid1"]]
        w1ld = {}
        w2ld = {}
        w1_lastmm = {}
        w2_lastmm = {}
        yev = {}

        def w1load(ee):
            deps = list(al.guards["w1_sb"]) if ee == 0 else [w1_lastmm[ee - 1]]
            return S.dma("wt", lambda e, ee=ee: e.dma_start(
                out=AP(w1_sb, 0, [[DC * F, 128], [F, DC], [1, F]]),
                in_=AP(w1.tensor, ee * D * F, [[F, 128], [128 * F, DC], [1, F]])),
                deps=deps)

        def w2load(ee):
            deps = list(al.guards["w2_sb"]) if ee == 0 else [w2_lastmm[ee - 1]]
            return S.dma("wt", lambda e, ee=ee: e.dma_start(
                out=AP(w2_sb, 0, [[FC * D, 128], [D, FC], [1, D]]),
                in_=AP(w2.tensor, ee * F * D, [[D, 128], [128 * D, FC], [1, D]])),
                deps=deps)

        w1ld[0] = w1load(0)
        w2ld[0] = w2load(0)
        prev_yev_half = {}
        last_w2mm = {}
        prev_w1ev = {}
        prev_gelu = {}
        prev_w1ev = {}
        prev_yev = [t_att_se, None]
        last_gelu_of_tt = {}
        prev_w1_lastev = None
        prev_any_yev = None
        for ee in range(E):
            if ee + 1 < E:
                w1ld[ee + 1] = None
            order = []
            for tt in range(4):
                order.append(("w1", tt))
                if tt >= 1:
                    order.append(("w2", tt - 1))
            order.append(("w2", 3))
            for kind, tt in order:
                if kind == "w1":
                    for fc in range(FC):
                        bidx = 2 + (fc % 2)
                        deps = [w1ld[ee],
                                h2ev[(4 * tt + 4) * DC - 1] if ee == 0 else t_h2fm]
                        key = (fc % 2,)
                        if key in prev_w1ev:
                            deps.append(prev_w1ev[key])
                        def w1mm(e, tt=tt, fc=fc, bidx=bidx):
                            for hc in range(DC):
                                ins = e.matmul(
                                    pb[bidx][:, 0:512],
                                    w1_sb[:, hc * F + fc * 128: hc * F + (fc + 1) * 128],
                                    hn2_fm[:, hc * L + tt * 512: hc * L + (tt + 1) * 512],
                                    start=(hc == 0), stop=(hc == DC - 1))
                            return ins
                        mmt = S.op("te", w1mm, deps=deps)
                        w1_lastmm[ee] = mmt
                        gdeps = [mmt]
                        if (tt % 2, fc) in prev_gelu:
                            gdeps.append(prev_gelu[(tt % 2, fc)])
                        if (tt % 2) in last_w2mm:
                            gdeps.append(last_w2mm[tt % 2])
                        if ee == 0 and tt < 2:
                            gdeps += gd_hm[tt % 2]
                        gt = S.op("se", lambda e, tt=tt, fc=fc, bidx=bidx: e.activation(
                            hmid[tt % 2][:, fc * 512:(fc + 1) * 512],
                            pb[bidx][:, 0:512], AF.Gelu), deps=gdeps)
                        prev_gelu[(tt % 2, fc)] = gt
                        prev_w1ev[(fc % 2,)] = gt
                        last_gelu_of_tt[tt] = gt
                else:
                    for tb in range(4 * tt, 4 * tt + 4):
                        for half in range(2):
                            bidx = 4 + half
                            deps = [w2ld[ee], last_gelu_of_tt[tt]]
                            if half in prev_yev_half:
                                deps.append(prev_yev_half[half])
                            def w2mm(e, tt=tt, tb=tb, half=half, bidx=bidx):
                                for fc in range(FC):
                                    ins = e.matmul(
                                        pb[bidx][:, 0:512],
                                        hmid[tt % 2][:, fc * 512 + (tb - 4 * tt) * 128:
                                                     fc * 512 + (tb - 4 * tt) * 128 + 128],
                                        w2_sb[:, fc * D + half * 512: fc * D + (half + 1) * 512],
                                        start=(fc == 0), stop=(fc == FC - 1))
                                return ins
                            mmt = S.op("te", w2mm, deps=deps)
                            w2_lastmm[ee] = mmt
                            last_w2mm[tt % 2] = mmt
                            ydeps = [mmt, t_gates]
                            if ee == 0:
                                ydeps += gd_ya
                            gap = gates[:, tb * E + ee: tb * E + ee + 1]
                            dst = ya(tb)[:, half * 512:(half + 1) * 512]
                            if ee == 0:
                                yv = S.op("ve", lambda e, dst=dst, gap=gap, bidx=bidx:
                                          e.tensor_scalar_mul(dst, pb[bidx][:, 0:512], gap),
                                          deps=ydeps)
                            else:
                                yv = S.op("ve", lambda e, dst=dst, gap=gap, bidx=bidx:
                                          e.scalar_tensor_tensor(
                                              dst, pb[bidx][:, 0:512], gap, dst,
                                              ALU.mult, ALU.add), deps=ydeps)
                            yev[(tb, half)] = yv
                            prev_yev_half[half] = yv
            if ee + 1 < E:
                w1ld[ee + 1] = w1load(ee + 1)
                w2ld[ee + 1] = w2load(ee + 1)
        t_yacc = yev[(NT - 1, 1)]
        al.drop("hn2_fm", after=[w1_lastmm[E - 1]])

        x1b = [al.alloc(f"x1b{i}", [128, D], F32) for i in range(2)]
        sb7 = [al.alloc(f"sb7{i}", [128, D], F32) for i in range(2)]
        xb7 = [al.alloc(f"xb7{i}", [128, D], F32) for i in range(2)]
        ot7 = [al.alloc(f"ot7{i}", [128, D], F32) for i in range(2)]
        al.drop("w1_sb", after=[w1_lastmm[E - 1]])
        al.drop("w2_sb", after=[w2_lastmm[E - 1]])
        al.drop("hmid0", "hmid1", after=[w2_lastmm[E - 1]])
        gd7 = {n: al.guards[n] for n in ("x1b0", "x1b1", "sb70", "sb71",
                                         "xb70", "xb71", "ot70", "ot71")}
        out_st = {}
        prev_ops = {}
        for tb in range(NT):
            war = [prev_ops[tb - 2]] if tb >= 2 else []
            t_l1 = S.dma("ld", lambda e, tb=tb: e.dma_start(
                out=x1b[tb % 2][:, :], in_=x1_sp[tb * 128:(tb + 1) * 128, :]),
                deps=[x1_store[tb]] + gd7[f"x1b{tb % 2}"] + war)
            t_ls = S.dma("kv", lambda e, tb=tb: e.dma_start(
                out=sb7[tb % 2][:, :], in_=s_sp[tb * 128:(tb + 1) * 128, :]),
                deps=[s_store[tb]] + gd7[f"sb7{tb % 2}"] + war)
            t_lx = S.dma("ld2", lambda e, tb=tb: e.dma_start(
                out=xb7[tb % 2][:, :], in_=x[tb * 128:(tb + 1) * 128, :]),
                deps=gd7[f"xb7{tb % 2}"] + war)
            f1 = S.op("ve", lambda e, tb=tb: e.tensor_add(
                x1b[tb % 2][:, :], x1b[tb % 2][:, :], ya(tb)),
                deps=[t_l1, yev[(tb, 0)], yev[(tb, 1)]])
            f2 = S.op("se", lambda e, tb=tb: e.activation(
                x1b[tb % 2][:, :], x1b[tb % 2][:, :], AF.Silu), deps=[f1])
            f3 = S.op("ve", lambda e, tb=tb: e.tensor_mul(
                x1b[tb % 2][:, :], x1b[tb % 2][:, :], sb7[tb % 2][:, :]),
                deps=[f2, t_ls])
            f4 = S.op("ve", lambda e, tb=tb: e.tensor_add(
                ot7[tb % 2][:, :], x1b[tb % 2][:, :], xb7[tb % 2][:, :]),
                deps=[f3, t_lx] + gd7[f"ot7{tb % 2}"] +
                     ([out_st[tb - 2]] if tb >= 2 else []))
            prev_ops[tb] = f4
            out_st[tb] = S.dma("st", lambda e, tb=tb: e.dma_start(
                out=out[tb * 128:(tb + 1) * 128, :], in_=ot7[tb % 2][:, :]),
                deps=[f4])
        return _finish(S, blk, dbg_marks, extra=[out_st[tb] for tb in range(NT)])


def _finish(S, blk, dbg_marks, extra=()):
    deps = [Tok(S.sem[e], S.cnt[e]) for e in ("te", "ve", "se", "ge") if S.cnt[e]]
    deps += [d for d in extra if d is not None]
    for name, (t, dram) in dbg_marks.items():
        S.dma("st", lambda e, t=t, dram=dram: e.dma_start(out=dram, in_=t[:, :]),
              deps=deps)

    def fin(e):
        for fam in ("st", "kv", "ld", "ld2", "wt"):
            q, sems, cnts = S.fam[fam]
            for sm, c in zip(sems, cnts):
                if c:
                    e.wait_ge(sm, 16 * c)
        if S.cc_cnt:
            e.wait_ge(S.cc_sem, S.cc_cnt)
        return e.nop()

    S.op("sy", fin, deps=deps, inc=False)
    S.emit(blk)
    return S.nc


def prep_inputs(inputs):
    g = lambda k: np.ascontiguousarray(np.asarray(inputs[k], dtype=np.float32))
    x = g("L0_attn_x_bld")
    shared = dict(
        wq=g("L0_attn_Wq").reshape(D, D),
        wkv=g("L0_attn_Wkv_down"),
        wkup=g("L0_attn_Wk_up").reshape(CL, D),
        wvup=g("L0_attn_Wv_up").reshape(CL, D),
        wout=g("L0_attn_Wout").reshape(D, D),
        wipT=np.ascontiguousarray(g("L0_ffn_in_proj_w").T),
        wopT=np.ascontiguousarray(g("L0_ffn_out_proj_w").T),
        wrT=np.ascontiguousarray(g("L0_ffn_router_w").T),
        w1=g("L0_ffn_w1"), w2=g("L0_ffn_w2"),
        ln1w=g("L0_ffn_ln1_w"), ln1b=g("L0_ffn_ln1_b"),
        ln2w=g("L0_ffn_ln2_w"), ln2b=g("L0_ffn_ln2_b"),
        ipb=g("L0_ffn_in_proj_b"), opb=g("L0_ffn_out_proj_b"))
    in_maps = []
    for c in range(8):
        m = dict(shared)
        m["x"] = np.ascontiguousarray(x[c])
        m["tmask"] = np.where(np.arange(E) <= c, 0.0, NEG).astype(np.float32)
        in_maps.append(m)
    return in_maps


def kernel(**inputs):
    in_maps = prep_inputs(inputs)
    nc = build_program("full")
    res = run_bass_kernel_spmd(nc, in_maps, core_ids=list(range(8)))
    return np.stack([np.asarray(r["out"]) for r in res.results])
